# version 33
# baseline (speedup 1.0000x reference)
import numpy as np
import ml_dtypes
from contextlib import ExitStack

import concourse.bass as bass
import concourse.mybir as mybir
from concourse.alu_op_type import AluOpType as A
from concourse.bass_utils import run_bass_kernel_spmd

F32 = mybir.dt.float32
BF16 = mybir.dt.bfloat16
AX = mybir.AxisListType.X
AF = mybir.ActivationFunctionType

N_CORES = 8
EPS = 1e-5
KW = 16


class Cfg:
    def __init__(self, T=2048, H=1024, I=1408, E=8):
        assert T % 512 == 0 and H % 128 == 0 and I % 128 == 0
        self.T, self.H, self.I, self.E = T, H, I, E
        self.TT = T // 128
        self.HC = H // 128
        self.IC = I // 128
        self.NSP = T // 512
        self.TH = T // 2
        self.NTB = self.TH // 128
        self.NTG = self.TH // 512
        self.HG = [(s, min(512, H - s)) for s in range(0, H, 512)]
        self.nst = len(self.HG)


class Prog:
    ENGINES = ("sync", "gpsimd", "tensor", "vector", "scalar")

    def __init__(self, sems):
        self.sems = sems
        self.count = {k: 0 for k in sems}
        self.ops = {e: [] for e in self.ENGINES}
        self.hwm = {e: {} for e in self.ENGINES}

    def add(self, engine, fn, waits=(), inc=None, inc_amt=1):
        wl = []
        for w in waits:
            if w is None:
                continue
            semname, val = w
            if val <= 0 or self.hwm[engine].get(semname, -1) >= val:
                continue
            self.hwm[engine][semname] = val
            wl.append((semname, val))
        ev = None
        if inc is not None:
            self.count[inc] += inc_amt
            ev = (inc, self.count[inc])
        self.ops[engine].append((fn, wl, (inc, inc_amt) if inc else None))
        return ev

    def drain(self, engine):
        self.ops[engine].append((lambda e: e.drain(), [], None))

    def replay(self, block):
        sems, ops = self.sems, self.ops

        def mk(engname):
            def body(eng):
                for fn, wl, inc in ops[engname]:
                    for semname, val in wl:
                        eng.wait_ge(sems[semname], val)
                    if fn is None:
                        continue
                    inst = fn(eng)
                    if inc is not None:
                        inst.then_inc(sems[inc[0]], inc[1])
            body.__name__ = "_"
            return body

        for engname in self.ENGINES:
            getattr(block, engname)(mk(engname))


def build_program(cfg: Cfg):
    nc = bass.Bass("TRN2", target_bir_lowering=False)
    T, H, I, E = cfg.T, cfg.H, cfg.I, cfg.E
    TT, HC, IC, NSP, TH = cfg.TT, cfg.HC, cfg.IC, cfg.NSP, cfg.TH
    NTB, NTG, HG, nst = cfg.NTB, cfg.NTG, cfg.HG, cfg.nst

    d_x = nc.dram_tensor("x", [T, H], F32, kind="ExternalInput").ap()
    d_w1 = nc.dram_tensor("w1", [H, H], F32, kind="ExternalInput").ap()
    d_w2 = nc.dram_tensor("w2", [H, H], F32, kind="ExternalInput").ap()
    d_vsp = nc.dram_tensor("vsp", [H, KW * KW], F32, kind="ExternalInput").ap()
    d_wrt = nc.dram_tensor("wrt", [H, E], F32, kind="ExternalInput").ap()
    d_wg = nc.dram_tensor("wg", [E, H, I], F32, kind="ExternalInput").ap()
    d_wu = nc.dram_tensor("wu", [E, H, I], F32, kind="ExternalInput").ap()
    d_wd = nc.dram_tensor("wd", [E, I, H], F32, kind="ExternalInput").ap()
    d_wgs = nc.dram_tensor("wgs", [H, I], F32, kind="ExternalInput").ap()
    d_wus = nc.dram_tensor("wus", [H, I], F32, kind="ExternalInput").ap()
    d_wds = nc.dram_tensor("wds", [I, H], F32, kind="ExternalInput").ap()
    d_iw = nc.dram_tensor("iw", [128, 448], F32, kind="ExternalInput").ap()
    d_id = nc.dram_tensor("ident", [128, 128], F32, kind="ExternalInput").ap()
    d_ones = nc.dram_tensor("ones", [128, 128], F32, kind="ExternalInput").ap()
    d_out = nc.dram_tensor("out", [T, H], F32, kind="ExternalOutput").ap()

    with ExitStack() as ctx:
        en = ctx.enter_context
        sems = {n: en(nc.semaphore(n)) for n in
                ["dx0", "dx1", "dx2", "dx3", "dc", "dgc", "dw", "dga0", "dga1", "dga2",
                 "dgb0", "dgb1", "dgb2", "dgd0", "dgd1", "dgd2", "do0", "do1",
                 "t", "v", "s", "g"]}
        P = Prog(sems)

        x2 = en(nc.sbuf_tensor("x2", [128, TT * H], F32))
        idf = en(nc.sbuf_tensor("idf", [128, 128], F32))
        idb = en(nc.sbuf_tensor("idb", [128, 128], BF16))
        onesb = en(nc.sbuf_tensor("onesb", [128, 128], BF16))
        vsp = en(nc.sbuf_tensor("vspb", [128, HC * KW * KW], BF16))
        wrt = en(nc.sbuf_tensor("wrtb", [128, HC * E], BF16))
        stat = {k: en(nc.sbuf_tensor("st_" + k, [128, TT], F32))
                for k in ["m1", "v1", "r1", "a1", "m2", "v2", "r2", "a2",
                          "m3", "v3", "r3", "a3", "m4", "v4", "r4", "a4"]}
        cw_all = en(nc.sbuf_tensor("cw_all", [128, TT * E], F32))
        bnb = en(nc.sbuf_tensor("bnb", [128, 24], F32))
        agg = en(nc.sbuf_tensor("agg", [128, 2], F32))
        agg2 = en(nc.sbuf_tensor("agg2", [128, 8], F32))
        smx = en(nc.sbuf_tensor("smx", [128, 32], F32))

        ev_xq = []
        for q in range(4):
            lo = q * (TT // 4)
            hi = (q + 1) * (TT // 4)
            ev_xq.append(P.add("sync", lambda e, lo=lo, hi=hi: e.dma_start(
                out=x2[:, lo * H: hi * H],
                in_=d_x.rearrange("(tt p) h -> p tt h", p=128)[:, lo:hi, :]),
                inc=f"dx{q}", inc_amt=16))
        ev_x = ev_xq[-1]
        ev_id = P.add("sync", lambda e: e.dma_start(out=idf[:], in_=d_id),
                      inc="dc", inc_amt=16)
        P.add("gpsimd", lambda e: e.dma_start(out=idb[:], in_=d_id),
              inc="dgc", inc_amt=16)
        P.add("gpsimd", lambda e: e.dma_start(out=onesb[:], in_=d_ones),
              inc="dgc", inc_amt=16)
        P.add("gpsimd", lambda e: e.dma_start(
            out=vsp[:], in_=d_vsp.rearrange("(c p) j -> p c j", p=128)),
            inc="dgc", inc_amt=16)
        ev_const = P.add("gpsimd", lambda e: e.dma_start(
            out=wrt[:], in_=d_wrt.rearrange("(c p) j -> p c j", p=128)),
            inc="dgc", inc_amt=16)
        ev_iw = ev_idb = ev_onesb = ev_vsp = ev_wrt = ev_const

        def ln_stats(src_fn, mkey, vkey, tau, waits):
            for ci, (s0, wdt) in enumerate(HG):
                P.add("vector", lambda e, ci=ci, s0=s0, wdt=wdt: e.bn_stats(
                    bnb[:, ci * 6:(ci + 1) * 6], src_fn(e, s0, wdt)), waits=waits)
                P.drain("vector")
            P.add("vector", lambda e: e.bn_aggr(agg[:], bnb[:, 0:6 * nst]))
            P.drain("vector")
            P.add("vector", lambda e, tau=tau: e.tensor_copy(
                stat[mkey][:, tau:tau + 1], agg[:, 0:1]))
            ev = P.add("vector", lambda e, tau=tau: e.tensor_copy(
                stat[vkey][:, tau:tau + 1], agg[:, 1:2]), inc="v")
            P.drain("vector")
            return ev

        def rstd_batch(mkey, vkey, rkey, akey, lo, hi, waits):
            eve_ = P.add("vector", lambda e: e.tensor_scalar(
                stat[rkey][:, lo:hi], stat[vkey][:, lo:hi], EPS, None, op0=A.add),
                waits=waits, inc="v")
            evq = P.add("scalar", lambda e: e.activation(
                stat[rkey][:, lo:hi], stat[rkey][:, lo:hi], AF.Sqrt),
                waits=[eve_], inc="s")
            P.add("vector", lambda e: e.reciprocal(
                stat[rkey][:, lo:hi], stat[rkey][:, lo:hi]), waits=[evq])
            P.drain("vector")
            P.add("vector", lambda e: e.tensor_scalar(
                stat[akey][:, lo:hi], stat[mkey][:, lo:hi], -1.0, None, op0=A.mult))
            P.drain("vector")
            ev = P.add("vector", lambda e: e.tensor_tensor(
                stat[akey][:, lo:hi], stat[akey][:, lo:hi], stat[rkey][:, lo:hi],
                op=A.mult), inc="v")
            P.drain("vector")
            return ev

        with ExitStack() as actx:
            aen = actx.enter_context
            xn = aen(nc.sbuf_tensor("xn", [128, TT * H], BF16))
            w1s = aen(nc.sbuf_tensor("w1s", [128, HC * H], BF16))
            w2s = aen(nc.sbuf_tensor("w2s", [128, HC * H], BF16))
            xnw = aen(nc.sbuf_tensor("xnw", [128, HC * 768], BF16))
            qw = aen(nc.sbuf_tensor("qw", [128, HC * 512], BF16))
            pjw = aen(nc.sbuf_tensor("pjw", [128, HC * 768], BF16))
            GW = min(4, HC)
            argb = [aen(nc.sbuf_tensor(f"argb{i}", [128, GW * 512], BF16)) for i in range(2)]
            tanb = [aen(nc.sbuf_tensor(f"tanb{i}", [128, GW * 512], BF16)) for i in range(2)]
            scf = aen(nc.sbuf_tensor("scf", [16, 512], F32))
            sct = aen(nc.sbuf_tensor("sct", [128, 64], F32))
            wts = aen(nc.sbuf_tensor("wts", [128, 4 * KW], F32))
            uwide = [aen(nc.sbuf_tensor(f"uwide{i}", [128, 384], BF16)) for i in range(2)]
            wtb = [aen(nc.sbuf_tensor("wtb0", [128, 384], BF16))] * 2
            bbuf = aen(nc.sbuf_tensor("bbuf", [128, 4 * H], F32))
            zscr = [aen(nc.sbuf_tensor("zscr0", [128, H], F32))] * 2
            pq = [aen(nc.psum_tensor(f"pq{i}", [128, 512], F32)) for i in range(2)]
            psc = aen(nc.psum_tensor("psc", [128, 512], F32))
            ptb = aen(nc.psum_tensor("ptb", [128, 1024], BF16))
            ptw = aen(nc.psum_tensor("ptw", [128, 1024], BF16))
            ptf = aen(nc.psum_tensor("ptf", [128, 192], F32))
            pB = [aen(nc.psum_tensor(f"pB{i}", [128, 512], F32)) for i in range(2)]

            P.add("gpsimd", lambda e: e.dma_start(
                out=w1s[:], in_=d_w1.rearrange("(c p) j -> p c j", p=128)),
                inc="dw", inc_amt=16)
            ev_w2 = P.add("gpsimd", lambda e: e.dma_start(
                out=w2s[:], in_=d_w2.rearrange("(c p) j -> p c j", p=128)),
                inc="dw", inc_amt=16)
            ev_w1 = ev_w2

            ev_v1 = None
            for tau in range(TT):
                ev_v1 = ln_stats(
                    lambda e, s0, wdt, tau=tau: x2[:, tau * H + s0: tau * H + s0 + wdt],
                    "m1", "v1", tau, [ev_xq[tau * 4 // TT]])
            ev_a1 = rstd_batch("m1", "v1", "r1", "a1", 0, TT, [ev_v1])
            ev_xn = [None] * TT
            for tau in range(TT):
                ev_xn[tau] = P.add("scalar", lambda e, tau=tau: e.activation(
                    xn[:, tau * H:(tau + 1) * H], x2[:, tau * H:(tau + 1) * H],
                    AF.Identity, bias=stat["a1"][:, tau:tau + 1],
                    scale=stat["r1"][:, tau:tau + 1]),
                    waits=[ev_a1], inc="s")

            ev_x2 = [None] * TT
            ev_tb_last = None
            ev_tw_last = None
            ev_tf_last = None
            ev_ptf_tail = None
            ev_qp_bankfree = [None, None]
            ev_pB_free = [None, None]
            ev_sc_prev = None
            argb_free = [None, None]
            tanb_free = [None, None]
            uw_free = [None, None]
            wtb_free = [None, None]
            prev_win_readers = []
            ev_s_sct = None
            ev_s_smx = None
            ev_s_bbuf = None
            zscr_free = [None]
            ev_q = ev_pj = ev_pjs = ev_last_add = None
            ev_sctv = None

            def emit_windows_qproj(sp, pump=None):
                nonlocal ev_tw_last, ev_q, ev_pj, ev_pjs, prev_win_readers, ev_ptf_tail
                b0 = sp * 4

                def pp(n=1):
                    if pump is not None:
                        for _ in range(n):
                            th_ = next(pump, None)
                            if th_ is not None:
                                th_()
                P.drain("vector")
                ev_roll = None
                if sp > 0:
                    for c in range(HC):
                        P.add("vector", lambda e, c=c: e.tensor_copy(
                            xnw[:, c * 768: c * 768 + 128],
                            xnw[:, c * 768 + 512: c * 768 + 640]),
                            waits=prev_win_readers)
                        ev_roll = P.add("vector", lambda e, c=c: e.tensor_copy(
                            xnw[:, c * 768 + 128: c * 768 + 256],
                            xnw[:, c * 768 + 640: c * 768 + 768]), inc="v")
                        ev_roll = P.add("vector", lambda e, c=c: e.tensor_copy(
                            pjw[:, c * 768: c * 768 + 128],
                            pjw[:, c * 768 + 512: c * 768 + 640]), inc="v")
                    P.drain("vector")
                    new_xn_slots = range(2, 6)
                else:
                    for c in range(HC):
                        P.add("vector", lambda e, c=c: e.memset(
                            xnw[:, c * 768: c * 768 + 256], 0.0))
                        ev_roll = P.add("vector", lambda e, c=c: e.memset(
                            pjw[:, c * 768: c * 768 + 128], 0.0), inc="v")
                    P.drain("vector")
                    new_xn_slots = range(1, 6)
                xnw3 = xnw.rearrange("p (c w) -> p c w", w=768)
                for s in new_xn_slots:
                    beta = b0 - 1 + s
                    if beta >= TT:
                        for c in range(HC):
                            ev_roll = P.add("vector", lambda e, c=c, s=s: e.memset(
                                xnw[:, c * 768 + s * 128: c * 768 + (s + 1) * 128],
                                0.0), inc="v")
                        P.drain("vector")
                        continue

                    def wtr(e, beta=beta):
                        last = None
                        for c in range(HC):
                            last = e.transpose(
                                ptw[:, c * 128:(c + 1) * 128],
                                xn[:, beta * H + c * 128: beta * H + (c + 1) * 128],
                                idb[:])
                        return last
                    evt = P.add("tensor", wtr,
                                waits=[ev_xn[beta], ev_idb, ev_tw_last, ev_roll],
                                inc="t")
                    ev_tw_last = P.add("vector", lambda e, s=s: e.tensor_copy(
                        xnw3[:, :, s * 128:(s + 1) * 128],
                        ptw[:, 0:HC * 128]), waits=[evt], inc="v")
                    pp(1)
                ev_win = ev_tw_last
                ev_slot_ms = ev_roll
                win_readers = []
                for c in range(HC):
                    bk = c % 2

                    def qmm(e, c=c, bk=bk):
                        last = None
                        for k in range(HC):
                            last = e.matmul(
                                pq[bk][:, 0:512],
                                w1s[:, k * H + c * 128: k * H + (c + 1) * 128],
                                xnw[:, k * 768 + 128: k * 768 + 640],
                                start=(k == 0), stop=(k == HC - 1))
                        return last
                    evm = P.add("tensor", qmm,
                                waits=[ev_w1, ev_win, ev_slot_ms,
                                       ev_qp_bankfree[bk]], inc="t")
                    win_readers.append(evm)
                    ev_q = P.add("scalar", lambda e, c=c, bk=bk: e.activation(
                        qw[:, c * 512:(c + 1) * 512], pq[bk][:, 0:512],
                        AF.Copy),
                        waits=[evm, ev_last_add], inc="s")
                    ev_qp_bankfree[bk] = ev_q
                    pp()
                for c in range(HC):
                    bk = c % 2

                    def pmm(e, c=c, bk=bk):
                        for k in range(HC):
                            e.matmul(
                                pq[bk][:, 0:512],
                                w2s[:, k * H + c * 128: k * H + (c + 1) * 128],
                                xnw[:, k * 768 + 128: k * 768 + 640],
                                start=(k == 0), stop=(k == HC - 1))
                        last = None
                        for k in range(HC):
                            last = e.matmul(
                                ptf[:, 64:192],
                                w2s[:, k * H + c * 128: k * H + (c + 1) * 128],
                                xnw[:, k * 768 + 640: k * 768 + 768],
                                start=(k == 0), stop=(k == HC - 1))
                        return last
                    evm = P.add("tensor", pmm,
                                waits=[ev_w2, ev_win, ev_slot_ms,
                                       ev_qp_bankfree[bk],
                                       ev_ptf_tail, ev_sctv], inc="t")
                    win_readers.append(evm)
                    evpjs = P.add("scalar", lambda e, c=c, bk=bk: e.activation(
                        pjw[:, c * 768 + 128: c * 768 + 640], pq[bk][:, 0:512],
                        AF.Copy),
                        waits=[evm, ev_last_add, ev_roll], inc="s")
                    win_readers.append(evpjs)
                    ev_pjs = evpjs
                    ev_pj = P.add("scalar", lambda e, c=c: e.activation(
                        pjw[:, c * 768 + 640: c * 768 + 768], ptf[:, 64:192],
                        AF.Copy),
                        waits=[evm, ev_last_add], inc="s")
                    ev_qp_bankfree[bk] = evpjs
                    ev_ptf_tail = ev_pj
                    pp()
                prev_win_readers = win_readers
                P.drain("vector")

            def emit_scores_softmax(sp, pump=None):
                nonlocal ev_sc_prev, ev_s_sct, ev_s_smx, ev_sctv, ev_tf_last
                nonlocal ev_last_add
                ev_q_sp, ev_pj_sp, ev_pjs_sp = ev_q, ev_pjs, ev_pj

                def spp(n=1):
                    if pump is not None:
                        for _ in range(n):
                            th_ = next(pump, None)
                            if th_ is not None:
                                th_()
                ev_sc = None
                nq = 0
                NQ = KW * (HC // GW)
                for k in range(KW):
                    for cq in range(HC // GW):
                        i = nq % 2
                        eva = None
                        for ci in range(GW):
                            c = cq * GW + ci
                            eva = P.add("vector", lambda e, k=k, c=c, ci=ci, i=i:
                                        e.tensor_tensor(
                                argb[i][:, ci * 512:(ci + 1) * 512],
                                pjw[:, c * 768 + 121 + k: c * 768 + 121 + k + 512],
                                qw[:, c * 512:(c + 1) * 512], op=A.add),
                                waits=[ev_q_sp, ev_pj_sp, ev_pjs_sp, argb_free[i]],
                                inc="v" if ci == GW - 1 else None)
                        evth = P.add("scalar", lambda e, i=i: e.activation(
                            tanb[i][:], argb[i][:], AF.Tanh),
                            waits=[eva, tanb_free[i]], inc="s")
                        argb_free[i] = evth
                        for ci in range(GW):
                            c = cq * GW + ci
                            ev_sc = P.add("tensor", lambda e, k=k, c=c, ci=ci, i=i,
                                          first=(nq == 0 and ci == 0),
                                          last=(nq == NQ - 1 and ci == GW - 1):
                                          e.matmul(
                                psc[0:16, 0:512],
                                vsp[:, c * KW * KW + k * KW:
                                    c * KW * KW + (k + 1) * KW],
                                tanb[i][:, ci * 512:(ci + 1) * 512],
                                start=first, stop=last),
                                waits=[evth, ev_vsp, ev_sc_prev],
                                inc="t" if ci == GW - 1 else None)
                        tanb_free[i] = ev_sc
                        nq += 1
                        spp(1)
                ev_last_add = eva
                ev_scev = P.add("vector", lambda e: e.tensor_copy(
                    scf[:], psc[0:16, 0:512]), waits=[ev_sc], inc="v")
                ev_sc_prev = ev_scev
                ev_sct = None
                for blk in range(4):
                    ev_sct = P.add("tensor", lambda e, blk=blk: e.transpose(
                        ptf[:, blk * 16:(blk + 1) * 16],
                        scf[:, blk * 128:(blk + 1) * 128], idf[0:16, 0:16]),
                        waits=[ev_scev, ev_id, ev_tf_last, ev_ptf_tail], inc="t")
                ev_sctv = P.add("vector", lambda e: e.tensor_copy(
                    sct[:], ptf[:, 0:64]), waits=[ev_sct, ev_s_sct], inc="v")
                ev_tf_last = ev_sctv
                P.drain("vector")
                for blk in range(4):
                    P.add("vector", lambda e, blk=blk: e.reduce_max(
                        smx[:, blk: blk + 1], sct[:, blk * 16:(blk + 1) * 16],
                        axis=AX), waits=[ev_sctv, ev_s_smx])
                P.drain("vector")
                evn = P.add("vector", lambda e: e.tensor_scalar(
                    smx[:, 4:8], smx[:, 0:4], -1.0, None, op0=A.mult), inc="v")
                P.drain("vector")
                eve = None
                for blk in range(4):
                    eve = P.add("scalar", lambda e, blk=blk: e.activation(
                        wts[:, blk * KW:(blk + 1) * KW],
                        sct[:, blk * 16:(blk + 1) * 16], AF.Exp,
                        bias=smx[:, 4 + blk: 5 + blk],
                        accum_out=smx[:, 8 + blk: 9 + blk]),
                        waits=[evn], inc="s")
                ev_s_sct = eve
                ev_s_smx = eve
                P.add("vector", lambda e: e.reciprocal(
                    smx[:, 12:16], smx[:, 8:12]), waits=[eve], inc="v")
                P.drain("vector")
                ev_wt = None
                for blk in range(4):
                    ev_wt = P.add("vector", lambda e, blk=blk: e.tensor_scalar(
                        wts[:, blk * KW:(blk + 1) * KW],
                        wts[:, blk * KW:(blk + 1) * KW],
                        smx[:, 12 + blk: 13 + blk], None, op0=A.mult), inc="v")
                P.drain("vector")
                return ev_wt

            def make_tail(sp, ev_wt):
                b0 = sp * 4
                state = {"ev_v2": None, "ev_bev": [None] * 4}

                def t_ub(blk, ui):
                    def f():
                        def ub(e, blk=blk, ui=ui):
                            e.memset(uwide[ui][:], 0.0)
                            last = None
                            for k in range(KW):
                                off = k + 121
                                wcol = wts[:, blk * KW + k: blk * KW + k + 1]
                                e.drain()
                                last = e.scalar_tensor_tensor(
                                    uwide[ui][:, off: off + 128], idb[:], wcol,
                                    uwide[ui][:, off: off + 128],
                                    op0=A.mult, op1=A.add)
                            return last
                        state[f"ub{blk}"] = P.add(
                            "vector", ub, waits=[ev_wt, ev_idb, uw_free[ui]],
                            inc="v")
                        P.drain("vector")
                    return f

                def t_band_tr(blk, tau, kaps, ui):
                    def f():
                        nonlocal ev_tb_last
                        wi = blk % 2

                        def utr(e, ui=ui, kaps=kaps, tau=tau):
                            last = None
                            for ki, kp in enumerate(kaps):
                                delta = kp - tau
                                last = e.transpose(
                                    ptb[:, ki * 128:(ki + 1) * 128],
                                    uwide[ui][:, (delta + 1) * 128:
                                               (delta + 2) * 128], idb[:])
                            return last
                        evt = P.add("tensor", utr,
                                    waits=[state[f"ub{blk}"], ev_tb_last],
                                    inc="t")
                        evw = P.add("vector", lambda e, wi=wi, n=len(kaps):
                                    e.tensor_copy(
                            wtb[wi][:, 0:n * 128], ptb[:, 0:n * 128]),
                            waits=[evt, wtb_free[wi]], inc="v")
                        ev_tb_last = evw
                        state[f"wt{blk}"] = evw
                    return f

                def t_band(blk, ki, kp, tau, nkaps, ui):
                    def f():
                        wi = blk % 2

                        def bmm(e, kp=kp, ki=ki, wi=wi, first=(ki == 0),
                                last=(ki == nkaps - 1)):
                            r = None
                            for gi, (s0_, wdt) in enumerate(HG):
                                r = e.matmul(pB[gi][:, 0:wdt],
                                             wtb[wi][:, ki * 128:(ki + 1) * 128],
                                             xn[:, kp * H + s0_: kp * H + s0_ + wdt],
                                             start=first, stop=last)
                            return r
                        ev_bmm = P.add("tensor", bmm,
                                       waits=[state[f"wt{blk}"], ev_xn[kp],
                                              ev_pB_free[0], ev_pB_free[1]],
                                       inc="t")
                        state[f"bmm{blk}"] = ev_bmm
                        if ki == nkaps - 1:
                            uw_free[ui] = ev_bmm
                            wtb_free[blk % 2] = ev_bmm
                    return f

                def t_stats(blk, tau):
                    def f():
                        nonlocal ev_s_bbuf
                        ev_bmm = state[f"bmm{blk}"]
                        for gi, (s0_, wdt) in enumerate(HG):
                            P.add("vector", lambda e, gi=gi, wdt=wdt: e.bn_stats(
                                bnb[:, gi * 6:(gi + 1) * 6], pB[gi][:, 0:wdt]),
                                waits=[ev_bmm])
                            P.drain("vector")
                        P.add("vector", lambda e: e.bn_aggr(
                            agg[:], bnb[:, 0:6 * nst]))
                        P.drain("vector")
                        P.add("vector", lambda e, tau=tau: e.tensor_copy(
                            stat["m2"][:, tau:tau + 1], agg[:, 0:1]))
                        state["ev_v2"] = P.add("vector", lambda e, tau=tau:
                                               e.tensor_copy(
                            stat["v2"][:, tau:tau + 1], agg[:, 1:2]), inc="v")
                        P.drain("vector")
                        for gi, (s0_, wdt) in enumerate(HG):
                            state["ev_bev"][blk] = P.add(
                                "vector", lambda e, blk=blk, s0_=s0_, wdt=wdt,
                                gi=gi: e.tensor_copy(
                                    bbuf[:, blk * H + s0_: blk * H + s0_ + wdt],
                                    pB[gi][:, 0:wdt]),
                                waits=[ev_s_bbuf], inc="v")
                            ev_pB_free[gi] = state["ev_bev"][blk]
                        P.drain("vector")
                    return f

                def t_rstd():
                    def f():
                        state["ev_a2"] = rstd_batch(
                            "m2", "v2", "r2", "a2", b0, b0 + 4, [state["ev_v2"]])
                    return f

                def t_x2(blk, tau):
                    def f():
                        nonlocal ev_s_bbuf
                        evz = P.add("scalar", lambda e, blk=blk, tau=tau:
                                    e.activation(
                            zscr[0][:], bbuf[:, blk * H:(blk + 1) * H],
                            AF.Identity,
                            bias=stat["a2"][:, tau:tau + 1],
                            scale=stat["r2"][:, tau:tau + 1]),
                            waits=[state["ev_a2"], state["ev_bev"][blk],
                                   zscr_free[0]], inc="s")
                        ev_s_bbuf = evz
                        ev_x2[tau] = P.add("vector", lambda e, tau=tau:
                                           e.tensor_tensor(
                            x2[:, tau * H:(tau + 1) * H],
                            x2[:, tau * H:(tau + 1) * H],
                            zscr[0][:], op=A.add), waits=[evz], inc="v")
                        zscr_free[0] = ev_x2[tau]
                        P.drain("vector")
                    return f

                per = []
                for blk in range(4):
                    tau = b0 + blk
                    kaps = [kp for kp in (tau - 1, tau, tau + 1) if 0 <= kp < TT]
                    ui = blk % 2
                    mm = [t_band(blk, ki, kp, tau, len(kaps), ui)
                          for ki, kp in enumerate(kaps)]
                    per.append((t_ub(blk, ui), t_band_tr(blk, tau, kaps, ui),
                                mm, t_stats(blk, tau)))
                thunks = [per[0][0], per[0][1], per[1][0]]
                for blk in range(4):
                    thunks.extend(per[blk][2])
                    if blk + 1 < 4:
                        thunks.append(per[blk + 1][1])
                    if blk + 2 < 4:
                        thunks.append(per[blk + 2][0])
                    thunks.append(per[blk][3])
                thunks.append(t_rstd())
                for blk in range(4):
                    thunks.append(t_x2(blk, b0 + blk))
                return iter(thunks)

            def warm_burst(n, waits=()):
                def wb(e, n=n):
                    last = None
                    for _ in range(n):
                        last = e.matmul(psc[:, 0:512], idb[:], argb[0][:, 0:512],
                                        start=True, stop=True)
                    return last
                return P.add("tensor", wb, waits=list(waits), inc="t")

            P.add("vector", lambda e: e.memset(argb[0][:], 0.0))
            P.drain("vector")
            ev_warm0 = P.add("vector", lambda e: e.memset(argb[1][:], 0.0), inc="v")
            ev_sc_prev = warm_burst(24, waits=[ev_idb, ev_warm0])

            emit_windows_qproj(0)
            tail = iter(())
            for sp in range(NSP):
                ev_wt = emit_scores_softmax(sp, pump=tail)
                for th_ in tail:
                    th_()
                tail = make_tail(sp, ev_wt)
                if sp + 1 < NSP:
                    emit_windows_qproj(sp + 1, pump=tail)
            for th_ in tail:
                th_()

        bar = [("t", P.count["t"]), ("v", P.count["v"]), ("s", P.count["s"])]

        with ExitStack() as mctx:
            men = mctx.enter_context
            hpr = men(nc.sbuf_tensor("hpr", [128, IC * TH], BF16))
            yfm = men(nc.sbuf_tensor("yfm", [128, HC * TH], F32))
            wgb = [men(nc.sbuf_tensor(f"wgb{i}", [128, HC * 128], BF16)) for i in range(3)]
            wub = [men(nc.sbuf_tensor(f"wub{i}", [128, HC * 128], BF16)) for i in range(3)]
            wdb = [men(nc.sbuf_tensor(f"wdb{i}", [128, IC * 128], BF16)) for i in range(3)]
            cwb2 = [men(nc.sbuf_tensor(f"cwb{i}", [128, TH], BF16)) for i in range(2)]
            sgb = men(nc.sbuf_tensor("sgb", [128, TH], BF16))
            t1b = men(nc.sbuf_tensor("t1b", [128, TH], BF16))
            lgs = men(nc.sbuf_tensor("lgs", [16, 512], F32))
            cwl = men(nc.sbuf_tensor("cwl", [128, 4 * E], F32))
            prb = men(nc.sbuf_tensor("prb", [128, E], F32))
            m8b = men(nc.sbuf_tensor("m8b", [128, 8], F32))
            kpb = men(nc.sbuf_tensor("kpb", [128, E], F32))
            Dsc8 = men(nc.sbuf_tensor("Dsc8", [128, NTB * 128], BF16))
            ytv = [men(nc.sbuf_tensor(f"ytv{i}", [128, H], F32)) for i in range(2)]
            ytv_chain = [None, None]
            osc = [men(nc.sbuf_tensor(f"osc{i}", [128, H], F32)) for i in range(2)]
            assert TH <= 1024 and H <= 1024
            pg = men(nc.psum_tensor("pg", [128, max(TH, H)], F32))
            pu = men(nc.psum_tensor("pu", [128, max(TH, H)], F32))
            py = [men(nc.psum_tensor(f"py{i}", [128, TH], F32)) for i in range(2)]

            xm_fm = men(nc.sbuf_tensor("xm_fm", [128, HC * T], BF16))
            xmt = [men(nc.sbuf_tensor("xmt0", [128, H], BF16))] * 2
            xmt_chain = [None]
            free_ev = {"pg": None, "pu": None, "py0": None, "py1": None}
            pbank = {"pg": pg, "pu": pu, "py0": py[0], "py1": py[1]}

            ev_xm_h = [None, None]
            ev_cw_h = [None, None]
            ev_pu_free0 = None
            xmt_free = [None, None]
            ln3_state = {"v3": None, "a3": [None, None]}

            def emit_ln3_stats(lo, hi, hf):
                for tau in range(lo, hi):
                    ln3_state["v3"] = ln_stats(
                        lambda e, s0, wdt, tau=tau:
                        x2[:, tau * H + s0: tau * H + s0 + wdt],
                        "m3", "v3", tau, [ev_x2[tau]])
                ln3_state["a3"][hf] = rstd_batch(
                    "m3", "v3", "r3", "a3", lo, hi, [ln3_state["v3"]])

            def emit_ln3_tile(tau, hf):
                nonlocal ev_pu_free0
                xi = tau % 2
                evx = P.add("scalar", lambda e, tau=tau, xi=xi: e.activation(
                    xmt[xi][:], x2[:, tau * H:(tau + 1) * H], AF.Identity,
                    bias=stat["a3"][:, tau:tau + 1],
                    scale=stat["r3"][:, tau:tau + 1]),
                    waits=[ln3_state["a3"][hf], xmt_chain[0]] + bar, inc="s")

                def xtr(e, xi=xi):
                    pub = pu[:, 0:512].bitcast(BF16)
                    last = None
                    for c in range(HC):
                        last = e.transpose(
                            pub[:, c * 128:(c + 1) * 128],
                            xmt[xi][:, c * 128:(c + 1) * 128], idb[:])
                    return last
                evt = P.add("tensor", xtr,
                            waits=[evx, ev_idb, ev_pu_free0, free_ev["pu"]],
                            inc="t")
                xm3 = xm_fm.rearrange("p (c t) -> p c t", t=T)
                ev_pu_free0 = P.add("vector", lambda e, tau=tau: e.tensor_copy(
                    xm3[:, :, tau * 128:(tau + 1) * 128],
                    pu[:, 0:512].bitcast(BF16)[:, 0:HC * 128]),
                    waits=[evt] + bar, inc="v")
                free_ev["pu"] = ev_pu_free0
                xmt_chain[0] = ev_pu_free0
                ev_xm_h[hf] = ev_pu_free0

            rt_state = {"lgs_free": None, "cwt": None, "s_smx2": None,
                        "s_cwl": None}

            def emit_router_mm(g):
                def lmm(e, g=g):
                    last = None
                    for k in range(HC):
                        last = e.matmul(
                            pg[0:E, 0:512], wrt[:, k * E:(k + 1) * E],
                            xm_fm[:, k * T + g * 512: k * T + (g + 1) * 512],
                            start=(k == 0), stop=(k == HC - 1))
                    return last
                hf = (g * 512) // TH
                evl = P.add("tensor", lmm,
                            waits=[ev_wrt, ev_xm_h[hf], rt_state["lgs_free"],
                                   free_ev["pg"]], inc="t")
                evlg = P.add("vector", lambda e: e.tensor_copy(
                    lgs[0:E, :], pg[0:E, 0:512]), waits=[evl], inc="v")
                free_ev["pg"] = evlg
                P.drain("vector")

                def rtr(e):
                    last = None
                    for j in range(4):
                        last = e.transpose(pu[:, j * E:(j + 1) * E],
                                           lgs[0:E, j * 128:(j + 1) * 128],
                                           idf[0:E, 0:E])
                    return last
                evt = P.add("tensor", rtr,
                            waits=[evlg, rt_state["cwt"], free_ev["pu"]],
                            inc="t")
                rt_state["lgs_free"] = evt
                evc = P.add("vector", lambda e: e.tensor_copy(
                    cwl[:], pu[:, 0:4 * E]), waits=[evt, rt_state["s_cwl"]],
                    inc="v")
                free_ev["pu"] = evc
                rt_state["cwt"] = evc
                P.drain("vector")

            def emit_router_cw(g, j, hf):
                tau = g * 4 + j
                P.add("vector", lambda e, j=j: e.reduce_max(
                    smx[:, 0:1], cwl[:, j * E:(j + 1) * E], axis=AX),
                    waits=[rt_state["cwt"]])
                P.drain("vector")
                evn = P.add("vector", lambda e: e.tensor_scalar(
                    smx[:, 1:2], smx[:, 0:1], -1.0, None, op0=A.mult),
                    waits=[rt_state["s_smx2"]], inc="v")
                eve = P.add("scalar", lambda e, j=j: e.activation(
                    prb[:], cwl[:, j * E:(j + 1) * E], AF.Exp, bias=smx[:, 1:2],
                    accum_out=smx[:, 2:3]), waits=[evn], inc="s")
                rt_state["s_cwl"] = eve
                rt_state["s_smx2"] = eve
                P.add("vector", lambda e: e.reciprocal(
                    smx[:, 3:4], smx[:, 2:3]), waits=[eve])
                P.drain("vector")
                P.add("vector", lambda e: e.tensor_scalar(
                    prb[:], prb[:], smx[:, 3:4], None, op0=A.mult))
                P.drain("vector")
                P.add("vector", lambda e: e.max(out=m8b[:], in_=prb[:]))
                P.drain("vector")
                P.add("vector", lambda e: e.tensor_scalar(
                    kpb[:], prb[:], m8b[:, 1:2], None, op0=A.is_ge))
                P.drain("vector")
                P.add("vector", lambda e: e.tensor_tensor(
                    kpb[:], kpb[:], prb[:], op=A.mult))
                P.drain("vector")
                P.add("vector", lambda e: e.tensor_tensor(
                    smx[:, 4:5], m8b[:, 0:1], m8b[:, 1:2], op=A.add))
                P.drain("vector")
                P.add("vector", lambda e: e.tensor_scalar(
                    smx[:, 4:5], smx[:, 4:5], 1e-20, None, op0=A.add))
                P.drain("vector")
                P.add("vector", lambda e: e.reciprocal(
                    smx[:, 5:6], smx[:, 4:5]), inc="v")
                P.drain("vector")
                ev = P.add("vector", lambda e, tau=tau: e.tensor_scalar(
                    cw_all[:, tau * E:(tau + 1) * E], kpb[:],
                    smx[:, 5:6], None, op0=A.mult), inc="v")
                P.drain("vector")
                ev_cw_h[hf] = ev

            def ln3_router_thunks(hf):
                lo = hf * NTB
                th_list = []
                th_list.append(lambda: emit_ln3_stats(lo, lo + NTB, hf))
                for tau in range(lo, lo + NTB):
                    th_list.append(lambda tau=tau: emit_ln3_tile(tau, hf))
                for g in range(lo // 4, (lo + NTB) // 4):
                    th_list.append(lambda g=g: emit_router_mm(g))
                    for j in range(4):
                        th_list.append(lambda g=g, j=j: emit_router_cw(g, j, hf))
                return th_list

            NE = E + 1
            wgb_free = [None, None, None]
            wub_free = [None, None, None]
            wdb_free = [None, None, None]
            gname = ["pg", "py0"]
            uname = ["pu", "py1"]
            pyname = ["py0", "py1"]
            ev_cwb_ready = [None, None]
            ev_cwb_cons2 = [None, None]
            ev_sgb_cons = None
            ev_t1b_cons = None
            ev_dsc_cons = None
            ev_out_dma = []
            nload = 0

            def emit_cwb_d(eidx, th):
                nonlocal ev_dsc_cons
                evd = None
                for blk in range(NTB):
                    tau = th * NTB + blk
                    evd = P.add("vector", lambda e, tau=tau, eidx=eidx, blk=blk:
                                e.tensor_scalar(
                        Dsc8[:, blk * 128:(blk + 1) * 128], idb[:],
                        cw_all[:, tau * E + eidx: tau * E + eidx + 1],
                        None, op0=A.mult),
                        waits=[ev_cw_h[th], ev_idb, ev_dsc_cons],
                        inc="v" if blk == NTB - 1 else None)
                return evd

            def emit_cwb_m(eidx, bankname, evd):
                nonlocal ev_dsc_cons
                bank = pbank[bankname]
                ri = eidx % 2

                def cwmm(e, bank=bank):
                    last = None
                    for blk in range(NTB):
                        last = e.matmul(
                            bank[:, blk * 128:(blk + 1) * 128], onesb[:],
                            Dsc8[:, blk * 128:(blk + 1) * 128],
                            start=True, stop=True)
                    return last
                evm = P.add("tensor", cwmm,
                            waits=[evd, ev_onesb, free_ev[bankname]], inc="t")
                ev_dsc_cons = evm
                ev = P.add("vector", lambda e, ri=ri, bank=bank: e.tensor_copy(
                    cwb2[ri][:], bank[:, 0:NTB * 128]),
                    waits=[evm, ev_cwb_cons2[ri]], inc="v")
                free_ev[bankname] = ev
                P.drain("vector")
                ev_cwb_ready[ri] = ev

            def emit_cwb(eidx, th, bankname):
                emit_cwb_m(eidx, bankname, emit_cwb_d(eidx, th))

            for th_fn in ln3_router_thunks(0):
                th_fn()
            moe_pump = iter(ln3_router_thunks(1))

            for th in range(2):
                tof = th * TH
                ev_y = [None] * HC
                if th == 1:
                    for th_fn in moe_pump:
                        th_fn()
                emit_cwb(0, th, "pg")
                for ei in range(NE):
                    routed = ei < E
                    ri = ei % 2

                    ev_h = None
                    evd_next = [None]
                    evld_pre = {}
                    for ic in range(IC):
                        sl = nload % 3
                        gn, un = gname[ic % 2], uname[ic % 2]
                        gb, ub = pbank[gn], pbank[un]
                        if routed:
                            src_g = d_wg[ei, :, ic * 128:(ic + 1) * 128]
                            src_u = d_wu[ei, :, ic * 128:(ic + 1) * 128]
                        else:
                            src_g = d_wgs[:, ic * 128:(ic + 1) * 128]
                            src_u = d_wus[:, ic * 128:(ic + 1) * 128]
                        evlg_ = P.add("gpsimd", lambda e, sl=sl, src_g=src_g: e.dma_start(
                            out=wgb[sl][:],
                            in_=src_g.rearrange("(c p) j -> p c j", p=128)),
                            waits=[wgb_free[sl]] + bar, inc=f"dga{sl}", inc_amt=16)
                        evlu_ = P.add("gpsimd", lambda e, sl=sl, src_u=src_u: e.dma_start(
                            out=wub[sl][:],
                            in_=src_u.rearrange("(c p) j -> p c j", p=128)),
                            waits=[wub_free[sl]] + bar, inc=f"dgb{sl}", inc_amt=16)
                        nload += 1

                        def gmm(e, sl=sl, tof=tof, gb=gb):
                            last = None
                            for c in range(HC):
                                for gg in range(NTG):
                                    last = e.matmul(
                                        gb[:, gg * 512:(gg + 1) * 512],
                                        wgb[sl][:, c * 128:(c + 1) * 128],
                                        xm_fm[:, c * T + tof + gg * 512:
                                              c * T + tof + (gg + 1) * 512],
                                        start=(c == 0), stop=(c == HC - 1))
                            return last
                        evg = P.add("tensor", gmm,
                                    waits=[evlg_, ev_xm_h[th], free_ev[gn]],
                                    inc="t")
                        wgb_free[sl] = evg

                        def umm(e, sl=sl, tof=tof, ub=ub):
                            last = None
                            for c in range(HC):
                                for gg in range(NTG):
                                    last = e.matmul(
                                        ub[:, gg * 512:(gg + 1) * 512],
                                        wub[sl][:, c * 128:(c + 1) * 128],
                                        xm_fm[:, c * T + tof + gg * 512:
                                              c * T + tof + (gg + 1) * 512],
                                        start=(c == 0), stop=(c == HC - 1))
                            return last
                        evu = P.add("tensor", umm,
                                    waits=[evlu_, ev_xm_h[th], free_ev[un]],
                                    inc="t")
                        wub_free[sl] = evu
                        evsg = P.add("scalar", lambda e, gb=gb: e.activation(
                            sgb[:], gb[:, 0:TH], AF.Silu),
                            waits=[evg, ev_sgb_cons], inc="s")
                        free_ev[gn] = evsg
                        evt1 = P.add("vector", lambda e, ub=ub: e.tensor_tensor(
                            t1b[:], sgb[:], ub[:, 0:TH], op=A.mult),
                            waits=[evsg, evu, ev_t1b_cons], inc="v")
                        ev_sgb_cons = evt1
                        free_ev[un] = evt1
                        P.drain("vector")
                        if routed:
                            ev_h = P.add("vector", lambda e, ic=ic, ri=ri:
                                         e.tensor_tensor(
                                hpr[:, ic * TH:(ic + 1) * TH], t1b[:], cwb2[ri][:],
                                op=A.mult),
                                waits=[evt1, ev_cwb_ready[ri]], inc="v")
                            ev_cwb_cons2[ri] = ev_h
                        else:
                            ev_h = P.add("vector", lambda e, ic=ic: e.tensor_copy(
                                hpr[:, ic * TH:(ic + 1) * TH], t1b[:]),
                                waits=[evt1], inc="v")
                        ev_t1b_cons = ev_h
                        P.drain("vector")
                        if ic == IC - 2 and routed and ei + 1 < E:
                            evd_next[0] = emit_cwb_d(ei + 1, th)
                        if IC >= 4 and ic >= IC - 3:
                            cpre = ic - (IC - 3)
                            if cpre < HC and cpre not in evld_pre:
                                sl2 = cpre % 3
                                if routed:
                                    src_d2 = d_wd[ei, :, cpre * 128:(cpre + 1) * 128]
                                else:
                                    src_d2 = d_wds[:, cpre * 128:(cpre + 1) * 128]
                                evld_pre[cpre] = P.add(
                                    "gpsimd", lambda e, sl2=sl2, src_d2=src_d2:
                                    e.dma_start(
                                        out=wdb[sl2][:],
                                        in_=src_d2.rearrange(
                                            "(ic p) j -> p ic j", p=128)),
                                    waits=[wdb_free[sl2]] + bar,
                                    inc=f"dgd{sl2}", inc_amt=16)

                    for c in range(HC):
                        sl = c % 3
                        pyn = pyname[c % 2]
                        if routed:
                            src_d = d_wd[ei, :, c * 128:(c + 1) * 128]
                        else:
                            src_d = d_wds[:, c * 128:(c + 1) * 128]
                        if c in evld_pre:
                            evld = evld_pre[c]
                        else:
                            evld = P.add("gpsimd", lambda e, sl=sl, src_d=src_d:
                                         e.dma_start(
                                out=wdb[sl][:],
                                in_=src_d.rearrange("(ic p) j -> p ic j", p=128)),
                                waits=[wdb_free[sl]] + bar, inc=f"dgd{sl}",
                                inc_amt=16)

                        def dmm(e, sl=sl, c=c, pyn=pyn):
                            last = None
                            for ic in range(IC):
                                for gg in range(NTG):
                                    last = e.matmul(
                                        pbank[pyn][:, gg * 512:(gg + 1) * 512],
                                        wdb[sl][:, ic * 128:(ic + 1) * 128],
                                        hpr[:, ic * TH + gg * 512:
                                            ic * TH + (gg + 1) * 512],
                                        start=(ic == 0), stop=(ic == IC - 1))
                            return last
                        evdm = P.add("tensor", dmm,
                                     waits=[evld, ev_h, free_ev[pyn]], inc="t")
                        wdb_free[sl] = evdm
                        if ei == 0:
                            ev_y[c] = P.add("vector", lambda e, c=c, pyn=pyn:
                                            e.tensor_copy(
                                yfm[:, c * TH:(c + 1) * TH],
                                pbank[pyn][:, 0:TH]),
                                waits=[evdm, ev_y[c]], inc="v")
                        else:
                            ev_y[c] = P.add("vector", lambda e, c=c, pyn=pyn:
                                            e.scalar_tensor_tensor(
                                yfm[:, c * TH:(c + 1) * TH], pbank[pyn][:, 0:TH],
                                1.0, yfm[:, c * TH:(c + 1) * TH],
                                op0=A.mult, op1=A.add),
                                waits=[evdm, ev_y[c]], inc="v")
                        free_ev[pyn] = ev_y[c]
                        P.drain("vector")
                        if c == 0 and routed and ei + 1 < E:
                            if evd_next[0] is None:
                                evd_next[0] = emit_cwb_d(ei + 1, th)
                            emit_cwb_m(ei + 1, "pg", evd_next[0])
                        if th == 0 and c in (2, 4, 6):
                            th_fn = next(moe_pump, None)
                            if th_fn is not None:
                                th_fn()

                for blk in range(NTB):
                    tau = th * NTB + blk
                    obk = "pg" if blk % 2 == 0 else "pu"
                    ob = pbank[obk]
                    ev_yt = None
                    for c in range(HC):
                        ev_yt = P.add("tensor", lambda e, c=c, blk=blk, ob=ob:
                                      e.transpose(
                            ob[:, c * 128:(c + 1) * 128],
                            yfm[:, c * TH + blk * 128: c * TH + (blk + 1) * 128],
                            idf[:]),
                            waits=[ev_y[c], ev_id, free_ev[obk]], inc="t")
                    free_ev[obk] = ev_yt
                    po = 4 * (blk % 2)
                    for gi in range((H + 511) // 512):
                        s0_ = gi * 512
                        wdt = min(512, H - s0_)
                        P.add("vector", lambda e, gi=gi, s0_=s0_, wdt=wdt, ob=ob,
                              blk=blk: e.bn_stats(
                            bnb[:, (blk % 2) * 12 + gi * 6:
                                (blk % 2) * 12 + (gi + 1) * 6],
                            ob[:, s0_: s0_ + wdt]),
                            waits=[ev_yt])
                        P.drain("vector")
                    P.add("vector", lambda e, po=po, blk=blk: e.bn_aggr(
                        agg2[:, po: po + 2],
                        bnb[:, (blk % 2) * 12: (blk % 2) * 12 + 6 * nst]))
                    P.drain("vector")
                    evv = P.add("vector", lambda e, po=po: e.tensor_scalar(
                        agg2[:, po + 2: po + 3], agg2[:, po + 1: po + 2], EPS,
                        None, op0=A.add), inc="v")
                    P.drain("vector")
                    evq = P.add("scalar", lambda e, po=po: e.activation(
                        agg2[:, po + 2: po + 3], agg2[:, po + 2: po + 3],
                        AF.Sqrt), waits=[evv], inc="s")
                    ev_a4 = P.add("vector", lambda e, po=po: e.reciprocal(
                        agg2[:, po + 3: po + 4], agg2[:, po + 2: po + 3]),
                        waits=[evq], inc="v")
                    P.drain("vector")
                    zi = blk % 2
                    evz = P.add("vector", lambda e, zi=zi, po=po, ob=ob:
                                e.tensor_scalar(
                        ytv[zi][:], ob[:, 0:H], agg2[:, po: po + 1],
                        agg2[:, po + 3: po + 4], op0=A.subtract, op1=A.mult),
                        waits=[ev_a4, ytv_chain[zi]], inc="v")
                    free_ev[obk] = evz
                    P.drain("vector")
                    evo = P.add("vector", lambda e, zi=zi, tau=tau: e.tensor_tensor(
                        osc[zi][:], ytv[zi][:], x2[:, tau * H:(tau + 1) * H],
                        op=A.add), waits=[evz] + ev_out_dma[-2:-1], inc="v")
                    ytv_chain[zi] = evo
                    P.drain("vector")
                    ev_out_dma.append(P.add("sync", lambda e, zi=zi, tau=tau: e.dma_start(
                        out=d_out.rearrange("(tt p) h -> p tt h", p=128)[
                            :, tau, :],
                        in_=osc[zi][:]), waits=[evo], inc=f"do{zi}", inc_amt=16))

            P.add("sync", None, waits=[("do0", P.count["do0"]),
                                       ("do1", P.count["do1"])])

        P.replay(en(nc.Block()))
    return nc


_NC_CACHE = {}


def _get_nc(key, cfg):
    if key not in _NC_CACHE:
        _NC_CACHE[key] = build_program(cfg)
    return _NC_CACHE[key]


def make_inputs(cfg, x_b, w1, w2, v, w_router, wg, wu, wd, wgs, wus, wds):
    H, E = cfg.H, cfg.E
    vsp = np.zeros((H, KW * KW), dtype=np.float32)
    for k in range(KW):
        vsp[:, k * KW + k] = v[:, 0]
    iw = np.zeros((128, 448), dtype=np.float32)
    for p in range(128):
        iw[p, p + 160] = 1.0
    ident = np.eye(128, dtype=np.float32)
    ones = np.ones((128, 128), dtype=np.float32)
    return {
        "x": np.ascontiguousarray(x_b),
        "w1": np.ascontiguousarray(w1),
        "w2": np.ascontiguousarray(w2),
        "vsp": vsp,
        "wrt": np.ascontiguousarray(w_router.T),
        "wg": np.ascontiguousarray(wg),
        "wu": np.ascontiguousarray(wu),
        "wd": np.ascontiguousarray(wd),
        "wgs": np.ascontiguousarray(wgs),
        "wus": np.ascontiguousarray(wus),
        "wds": np.ascontiguousarray(wds),
        "iw": iw,
        "ident": ident,
        "ones": ones,
    }


def kernel(L0_attn_inputs, L0_attn_w1, L0_attn_w2, L0_attn_v,
           L0_ffn_w_router, L0_ffn_wg, L0_ffn_wu, L0_ffn_wd,
           L0_ffn_wg_s, L0_ffn_wu_s, L0_ffn_wd_s):
    x = np.asarray(L0_attn_inputs, dtype=np.float32)
    B, T, H = x.shape
    cfg = Cfg(T=T, H=H, I=np.asarray(L0_ffn_wg).shape[2],
              E=np.asarray(L0_ffn_w_router).shape[0])
    nc = _get_nc((T, H, cfg.I, cfg.E), cfg)
    common = dict(
        w1=np.asarray(L0_attn_w1, np.float32), w2=np.asarray(L0_attn_w2, np.float32),
        v=np.asarray(L0_attn_v, np.float32),
        w_router=np.asarray(L0_ffn_w_router, np.float32),
        wg=np.asarray(L0_ffn_wg, np.float32), wu=np.asarray(L0_ffn_wu, np.float32),
        wd=np.asarray(L0_ffn_wd, np.float32),
        wgs=np.asarray(L0_ffn_wg_s, np.float32),
        wus=np.asarray(L0_ffn_wu_s, np.float32),
        wds=np.asarray(L0_ffn_wd_s, np.float32))
    in_maps = [make_inputs(cfg, x[b], **common) for b in range(B)]
    res = run_bass_kernel_spmd(nc, in_maps, core_ids=list(range(B)))
    out = np.stack([np.asarray(res.results[b]["out"]) for b in range(B)], axis=0)
    return out.astype(np.float32)


# version 34
# speedup vs baseline: 1.0050x; 1.0050x over previous
import numpy as np
import ml_dtypes
from contextlib import ExitStack

import concourse.bass as bass
import concourse.mybir as mybir
from concourse.alu_op_type import AluOpType as A
from concourse.bass_utils import run_bass_kernel_spmd

F32 = mybir.dt.float32
BF16 = mybir.dt.bfloat16
AX = mybir.AxisListType.X
AF = mybir.ActivationFunctionType

N_CORES = 8
EPS = 1e-5
KW = 16


class Cfg:
    def __init__(self, T=2048, H=1024, I=1408, E=8):
        assert T % 512 == 0 and H % 128 == 0 and I % 128 == 0
        self.T, self.H, self.I, self.E = T, H, I, E
        self.TT = T // 128
        self.HC = H // 128
        self.IC = I // 128
        self.NSP = T // 512
        self.TH = T // 2
        self.NTB = self.TH // 128
        self.NTG = self.TH // 512
        self.HG = [(s, min(512, H - s)) for s in range(0, H, 512)]
        self.nst = len(self.HG)


class Prog:
    ENGINES = ("sync", "gpsimd", "tensor", "vector", "scalar")

    def __init__(self, sems):
        self.sems = sems
        self.count = {k: 0 for k in sems}
        self.ops = {e: [] for e in self.ENGINES}
        self.hwm = {e: {} for e in self.ENGINES}

    def add(self, engine, fn, waits=(), inc=None, inc_amt=1):
        wl = []
        for w in waits:
            if w is None:
                continue
            semname, val = w
            if val <= 0 or self.hwm[engine].get(semname, -1) >= val:
                continue
            self.hwm[engine][semname] = val
            wl.append((semname, val))
        ev = None
        if inc is not None:
            self.count[inc] += inc_amt
            ev = (inc, self.count[inc])
        self.ops[engine].append((fn, wl, (inc, inc_amt) if inc else None))
        return ev

    def drain(self, engine):
        self.ops[engine].append((lambda e: e.drain(), [], None))

    def replay(self, block):
        sems, ops = self.sems, self.ops

        def mk(engname):
            def body(eng):
                for fn, wl, inc in ops[engname]:
                    for semname, val in wl:
                        eng.wait_ge(sems[semname], val)
                    if fn is None:
                        continue
                    inst = fn(eng)
                    if inc is not None:
                        inst.then_inc(sems[inc[0]], inc[1])
            body.__name__ = "_"
            return body

        for engname in self.ENGINES:
            getattr(block, engname)(mk(engname))


def build_program(cfg: Cfg):
    nc = bass.Bass("TRN2", target_bir_lowering=False)
    T, H, I, E = cfg.T, cfg.H, cfg.I, cfg.E
    TT, HC, IC, NSP, TH = cfg.TT, cfg.HC, cfg.IC, cfg.NSP, cfg.TH
    NTB, NTG, HG, nst = cfg.NTB, cfg.NTG, cfg.HG, cfg.nst

    d_x = nc.dram_tensor("x", [T, H], F32, kind="ExternalInput").ap()
    d_w1 = nc.dram_tensor("w1", [H, H], F32, kind="ExternalInput").ap()
    d_w2 = nc.dram_tensor("w2", [H, H], F32, kind="ExternalInput").ap()
    d_vsp = nc.dram_tensor("vsp", [H, KW * KW], F32, kind="ExternalInput").ap()
    d_wrt = nc.dram_tensor("wrt", [H, E], F32, kind="ExternalInput").ap()
    d_wg = nc.dram_tensor("wg", [E, H, I], F32, kind="ExternalInput").ap()
    d_wu = nc.dram_tensor("wu", [E, H, I], F32, kind="ExternalInput").ap()
    d_wd = nc.dram_tensor("wd", [E, I, H], F32, kind="ExternalInput").ap()
    d_wgs = nc.dram_tensor("wgs", [H, I], F32, kind="ExternalInput").ap()
    d_wus = nc.dram_tensor("wus", [H, I], F32, kind="ExternalInput").ap()
    d_wds = nc.dram_tensor("wds", [I, H], F32, kind="ExternalInput").ap()
    d_iw = nc.dram_tensor("iw", [128, 448], F32, kind="ExternalInput").ap()
    d_id = nc.dram_tensor("ident", [128, 128], F32, kind="ExternalInput").ap()
    d_ones = nc.dram_tensor("ones", [128, 128], F32, kind="ExternalInput").ap()
    d_out = nc.dram_tensor("out", [T, H], F32, kind="ExternalOutput").ap()

    with ExitStack() as ctx:
        en = ctx.enter_context
        sems = {n: en(nc.semaphore(n)) for n in
                ["dx0", "dx1", "dx2", "dx3", "dc", "dgc", "dw", "dga0", "dga1", "dga2",
                 "dgb0", "dgb1", "dgb2", "dgd0", "dgd1", "dgd2", "do0", "do1",
                 "t", "v", "s", "g"]}
        P = Prog(sems)

        x2 = en(nc.sbuf_tensor("x2", [128, TT * H], F32))
        idf = en(nc.sbuf_tensor("idf", [128, 128], F32))
        idb = en(nc.sbuf_tensor("idb", [128, 128], BF16))
        onesb = en(nc.sbuf_tensor("onesb", [128, 128], BF16))
        vsp = en(nc.sbuf_tensor("vspb", [128, HC * KW * KW], BF16))
        wrt = en(nc.sbuf_tensor("wrtb", [128, HC * E], BF16))
        stat = {k: en(nc.sbuf_tensor("st_" + k, [128, TT], F32))
                for k in ["m1", "v1", "r1", "a1", "m2", "v2", "r2", "a2",
                          "m3", "v3", "r3", "a3", "m4", "v4", "r4", "a4"]}
        cw_all = en(nc.sbuf_tensor("cw_all", [128, TT * E], F32))
        bnb = en(nc.sbuf_tensor("bnb", [128, 24], F32))
        agg = en(nc.sbuf_tensor("agg", [128, 2], F32))
        agg2 = en(nc.sbuf_tensor("agg2", [128, 8], F32))
        smx = en(nc.sbuf_tensor("smx", [128, 32], F32))

        ev_xq = []
        for q in range(4):
            lo = q * (TT // 4)
            hi = (q + 1) * (TT // 4)
            ev_xq.append(P.add("sync", lambda e, lo=lo, hi=hi: e.dma_start(
                out=x2[:, lo * H: hi * H],
                in_=d_x.rearrange("(tt p) h -> p tt h", p=128)[:, lo:hi, :]),
                inc=f"dx{q}", inc_amt=16))
        ev_x = ev_xq[-1]
        ev_id = P.add("sync", lambda e: e.dma_start(out=idf[:], in_=d_id),
                      inc="dc", inc_amt=16)
        P.add("gpsimd", lambda e: e.dma_start(out=idb[:], in_=d_id),
              inc="dgc", inc_amt=16)
        P.add("gpsimd", lambda e: e.dma_start(out=onesb[:], in_=d_ones),
              inc="dgc", inc_amt=16)
        P.add("gpsimd", lambda e: e.dma_start(
            out=vsp[:], in_=d_vsp.rearrange("(c p) j -> p c j", p=128)),
            inc="dgc", inc_amt=16)
        ev_const = P.add("gpsimd", lambda e: e.dma_start(
            out=wrt[:], in_=d_wrt.rearrange("(c p) j -> p c j", p=128)),
            inc="dgc", inc_amt=16)
        ev_iw = ev_idb = ev_onesb = ev_vsp = ev_wrt = ev_const

        def ln_stats(src_fn, mkey, vkey, tau, waits):
            for ci, (s0, wdt) in enumerate(HG):
                P.add("vector", lambda e, ci=ci, s0=s0, wdt=wdt: e.bn_stats(
                    bnb[:, ci * 6:(ci + 1) * 6], src_fn(e, s0, wdt)), waits=waits)
                P.drain("vector")
            P.add("vector", lambda e: e.bn_aggr(agg[:], bnb[:, 0:6 * nst]))
            P.drain("vector")
            P.add("vector", lambda e, tau=tau: e.tensor_copy(
                stat[mkey][:, tau:tau + 1], agg[:, 0:1]))
            ev = P.add("vector", lambda e, tau=tau: e.tensor_copy(
                stat[vkey][:, tau:tau + 1], agg[:, 1:2]), inc="v")
            P.drain("vector")
            return ev

        def rstd_batch(mkey, vkey, rkey, akey, lo, hi, waits):
            eve_ = P.add("vector", lambda e: e.tensor_scalar(
                stat[rkey][:, lo:hi], stat[vkey][:, lo:hi], EPS, None, op0=A.add),
                waits=waits, inc="v")
            evq = P.add("scalar", lambda e: e.activation(
                stat[rkey][:, lo:hi], stat[rkey][:, lo:hi], AF.Sqrt),
                waits=[eve_], inc="s")
            P.add("vector", lambda e: e.reciprocal(
                stat[rkey][:, lo:hi], stat[rkey][:, lo:hi]), waits=[evq])
            P.drain("vector")
            P.add("vector", lambda e: e.tensor_scalar(
                stat[akey][:, lo:hi], stat[mkey][:, lo:hi], -1.0, None, op0=A.mult))
            P.drain("vector")
            ev = P.add("vector", lambda e: e.tensor_tensor(
                stat[akey][:, lo:hi], stat[akey][:, lo:hi], stat[rkey][:, lo:hi],
                op=A.mult), inc="v")
            P.drain("vector")
            return ev

        with ExitStack() as actx:
            aen = actx.enter_context
            xn = aen(nc.sbuf_tensor("xn", [128, TT * H], BF16))
            w1s = aen(nc.sbuf_tensor("w1s", [128, HC * H], BF16))
            w2s = aen(nc.sbuf_tensor("w2s", [128, HC * H], BF16))
            xnw = aen(nc.sbuf_tensor("xnw", [128, HC * 768], BF16))
            qw = aen(nc.sbuf_tensor("qw", [128, HC * 512], BF16))
            pjw = aen(nc.sbuf_tensor("pjw", [128, HC * 768], BF16))
            GW = min(4, HC)
            argb = [aen(nc.sbuf_tensor(f"argb{i}", [128, GW * 512], BF16)) for i in range(2)]
            tanb = [aen(nc.sbuf_tensor(f"tanb{i}", [128, GW * 512], BF16)) for i in range(2)]
            scf = aen(nc.sbuf_tensor("scf", [16, 512], F32))
            sct = aen(nc.sbuf_tensor("sct", [128, 64], F32))
            wts = aen(nc.sbuf_tensor("wts", [128, 4 * KW], F32))
            uwide = [aen(nc.sbuf_tensor(f"uwide{i}", [128, 384], BF16)) for i in range(2)]
            wtb = [aen(nc.sbuf_tensor("wtb0", [128, 384], BF16))] * 2
            bbuf = aen(nc.sbuf_tensor("bbuf", [128, 4 * H], F32))
            zscr = [aen(nc.sbuf_tensor("zscr0", [128, H], F32))] * 2
            pq = [aen(nc.psum_tensor(f"pq{i}", [128, 512], F32)) for i in range(2)]
            psc = aen(nc.psum_tensor("psc", [128, 512], F32))
            ptb = aen(nc.psum_tensor("ptb", [128, 1024], BF16))
            ptw = aen(nc.psum_tensor("ptw", [128, 1024], BF16))
            ptf = aen(nc.psum_tensor("ptf", [128, 192], F32))
            pB = [aen(nc.psum_tensor(f"pB{i}", [128, 512], F32)) for i in range(2)]

            P.add("gpsimd", lambda e: e.dma_start(
                out=w1s[:], in_=d_w1.rearrange("(c p) j -> p c j", p=128)),
                inc="dw", inc_amt=16)
            ev_w2 = P.add("gpsimd", lambda e: e.dma_start(
                out=w2s[:], in_=d_w2.rearrange("(c p) j -> p c j", p=128)),
                inc="dw", inc_amt=16)
            ev_w1 = ev_w2

            ev_v1 = None
            for tau in range(TT):
                ev_v1 = ln_stats(
                    lambda e, s0, wdt, tau=tau: x2[:, tau * H + s0: tau * H + s0 + wdt],
                    "m1", "v1", tau, [ev_xq[tau * 4 // TT]])
            ev_a1 = rstd_batch("m1", "v1", "r1", "a1", 0, TT, [ev_v1])
            ev_xn = [None] * TT
            for tau in range(TT):
                ev_xn[tau] = P.add("scalar", lambda e, tau=tau: e.activation(
                    xn[:, tau * H:(tau + 1) * H], x2[:, tau * H:(tau + 1) * H],
                    AF.Identity, bias=stat["a1"][:, tau:tau + 1],
                    scale=stat["r1"][:, tau:tau + 1]),
                    waits=[ev_a1], inc="s")

            ev_x2 = [None] * TT
            ev_tb_last = None
            ev_tw_last = None
            ev_tf_last = None
            ev_ptf_tail = None
            ev_qp_bankfree = [None, None]
            ev_pB_free = [None, None]
            ev_sc_prev = None
            argb_free = [None, None]
            tanb_free = [None, None]
            uw_free = [None, None]
            wtb_free = [None, None]
            prev_win_readers = []
            ev_s_sct = None
            ev_s_smx = None
            ev_s_bbuf = None
            zscr_free = [None]
            ev_q = ev_pj = ev_pjs = ev_last_add = None
            ev_sctv = None

            def emit_windows_qproj(sp, pump=None):
                nonlocal ev_tw_last, ev_q, ev_pj, ev_pjs, prev_win_readers, ev_ptf_tail
                b0 = sp * 4

                def pp(n=1):
                    if pump is not None:
                        for _ in range(n):
                            th_ = next(pump, None)
                            if th_ is not None:
                                th_()
                P.drain("vector")
                ev_roll = None
                if sp > 0:
                    for c in range(HC):
                        P.add("vector", lambda e, c=c: e.tensor_copy(
                            xnw[:, c * 768: c * 768 + 128],
                            xnw[:, c * 768 + 512: c * 768 + 640]),
                            waits=prev_win_readers)
                        ev_roll = P.add("vector", lambda e, c=c: e.tensor_copy(
                            xnw[:, c * 768 + 128: c * 768 + 256],
                            xnw[:, c * 768 + 640: c * 768 + 768]), inc="v")
                        ev_roll = P.add("vector", lambda e, c=c: e.tensor_copy(
                            pjw[:, c * 768: c * 768 + 128],
                            pjw[:, c * 768 + 512: c * 768 + 640]), inc="v")
                    P.drain("vector")
                    new_xn_slots = range(2, 6)
                else:
                    for c in range(HC):
                        P.add("vector", lambda e, c=c: e.memset(
                            xnw[:, c * 768: c * 768 + 256], 0.0))
                        ev_roll = P.add("vector", lambda e, c=c: e.memset(
                            pjw[:, c * 768: c * 768 + 128], 0.0), inc="v")
                    P.drain("vector")
                    new_xn_slots = range(1, 6)
                xnw3 = xnw.rearrange("p (c w) -> p c w", w=768)
                for s in new_xn_slots:
                    beta = b0 - 1 + s
                    if beta >= TT:
                        for c in range(HC):
                            ev_roll = P.add("vector", lambda e, c=c, s=s: e.memset(
                                xnw[:, c * 768 + s * 128: c * 768 + (s + 1) * 128],
                                0.0), inc="v")
                        P.drain("vector")
                        continue

                    def wtr(e, beta=beta):
                        last = None
                        for c in range(HC):
                            last = e.transpose(
                                ptw[:, c * 128:(c + 1) * 128],
                                xn[:, beta * H + c * 128: beta * H + (c + 1) * 128],
                                idb[:])
                        return last
                    evt = P.add("tensor", wtr,
                                waits=[ev_xn[beta], ev_idb, ev_tw_last, ev_roll],
                                inc="t")
                    ev_tw_last = P.add("scalar", lambda e, s=s: e.activation(
                        xnw3[:, :, s * 128:(s + 1) * 128],
                        ptw[:, 0:HC * 128], AF.Copy), waits=[evt], inc="s")
                    pp(1)
                ev_win = ev_tw_last
                ev_slot_ms = ev_roll
                win_readers = []
                for c in range(HC):
                    bk = c % 2

                    def qmm(e, c=c, bk=bk):
                        last = None
                        for k in range(HC):
                            last = e.matmul(
                                pq[bk][:, 0:512],
                                w1s[:, k * H + c * 128: k * H + (c + 1) * 128],
                                xnw[:, k * 768 + 128: k * 768 + 640],
                                start=(k == 0), stop=(k == HC - 1))
                        return last
                    evm = P.add("tensor", qmm,
                                waits=[ev_w1, ev_win, ev_slot_ms,
                                       ev_qp_bankfree[bk]], inc="t")
                    win_readers.append(evm)
                    ev_q = P.add("scalar", lambda e, c=c, bk=bk: e.activation(
                        qw[:, c * 512:(c + 1) * 512], pq[bk][:, 0:512],
                        AF.Copy),
                        waits=[evm, ev_last_add], inc="s")
                    ev_qp_bankfree[bk] = ev_q
                    pp()
                for c in range(HC):
                    bk = c % 2

                    def pmm(e, c=c, bk=bk):
                        for k in range(HC):
                            e.matmul(
                                pq[bk][:, 0:512],
                                w2s[:, k * H + c * 128: k * H + (c + 1) * 128],
                                xnw[:, k * 768 + 128: k * 768 + 640],
                                start=(k == 0), stop=(k == HC - 1))
                        last = None
                        for k in range(HC):
                            last = e.matmul(
                                ptf[:, 64:192],
                                w2s[:, k * H + c * 128: k * H + (c + 1) * 128],
                                xnw[:, k * 768 + 640: k * 768 + 768],
                                start=(k == 0), stop=(k == HC - 1))
                        return last
                    evm = P.add("tensor", pmm,
                                waits=[ev_w2, ev_win, ev_slot_ms,
                                       ev_qp_bankfree[bk],
                                       ev_ptf_tail, ev_sctv], inc="t")
                    win_readers.append(evm)
                    evpjs = P.add("scalar", lambda e, c=c, bk=bk: e.activation(
                        pjw[:, c * 768 + 128: c * 768 + 640], pq[bk][:, 0:512],
                        AF.Copy),
                        waits=[evm, ev_last_add, ev_roll], inc="s")
                    win_readers.append(evpjs)
                    ev_pjs = evpjs
                    ev_pj = P.add("scalar", lambda e, c=c: e.activation(
                        pjw[:, c * 768 + 640: c * 768 + 768], ptf[:, 64:192],
                        AF.Copy),
                        waits=[evm, ev_last_add], inc="s")
                    ev_qp_bankfree[bk] = evpjs
                    ev_ptf_tail = ev_pj
                    pp()
                prev_win_readers = win_readers
                P.drain("vector")

            def emit_scores_softmax(sp, pump=None):
                nonlocal ev_sc_prev, ev_s_sct, ev_s_smx, ev_sctv, ev_tf_last
                nonlocal ev_last_add
                ev_q_sp, ev_pj_sp, ev_pjs_sp = ev_q, ev_pjs, ev_pj

                def spp(n=1):
                    if pump is not None:
                        for _ in range(n):
                            th_ = next(pump, None)
                            if th_ is not None:
                                th_()
                ev_sc = None
                nq = 0
                NQ = KW * (HC // GW)
                for k in range(KW):
                    for cq in range(HC // GW):
                        i = nq % 2
                        eva = None
                        for ci in range(GW):
                            c = cq * GW + ci
                            eva = P.add("vector", lambda e, k=k, c=c, ci=ci, i=i:
                                        e.tensor_tensor(
                                argb[i][:, ci * 512:(ci + 1) * 512],
                                pjw[:, c * 768 + 121 + k: c * 768 + 121 + k + 512],
                                qw[:, c * 512:(c + 1) * 512], op=A.add),
                                waits=[ev_q_sp, ev_pj_sp, ev_pjs_sp, argb_free[i]],
                                inc="v" if ci == GW - 1 else None)
                        evth = P.add("scalar", lambda e, i=i: e.activation(
                            tanb[i][:], argb[i][:], AF.Tanh),
                            waits=[eva, tanb_free[i]], inc="s")
                        argb_free[i] = evth
                        for ci in range(GW):
                            c = cq * GW + ci
                            ev_sc = P.add("tensor", lambda e, k=k, c=c, ci=ci, i=i,
                                          first=(nq == 0 and ci == 0),
                                          last=(nq == NQ - 1 and ci == GW - 1):
                                          e.matmul(
                                psc[0:16, 0:512],
                                vsp[:, c * KW * KW + k * KW:
                                    c * KW * KW + (k + 1) * KW],
                                tanb[i][:, ci * 512:(ci + 1) * 512],
                                start=first, stop=last),
                                waits=[evth, ev_vsp, ev_sc_prev],
                                inc="t" if ci == GW - 1 else None)
                        tanb_free[i] = ev_sc
                        nq += 1
                        spp(1)
                ev_last_add = eva
                ev_scev = P.add("vector", lambda e: e.tensor_copy(
                    scf[:], psc[0:16, 0:512]), waits=[ev_sc], inc="v")
                ev_sc_prev = ev_scev
                ev_sct = None
                for blk in range(4):
                    ev_sct = P.add("tensor", lambda e, blk=blk: e.transpose(
                        ptf[:, blk * 16:(blk + 1) * 16],
                        scf[:, blk * 128:(blk + 1) * 128], idf[0:16, 0:16]),
                        waits=[ev_scev, ev_id, ev_tf_last, ev_ptf_tail], inc="t")
                ev_sctv = P.add("vector", lambda e: e.tensor_copy(
                    sct[:], ptf[:, 0:64]), waits=[ev_sct, ev_s_sct], inc="v")
                ev_tf_last = ev_sctv
                P.drain("vector")
                for blk in range(4):
                    P.add("vector", lambda e, blk=blk: e.reduce_max(
                        smx[:, blk: blk + 1], sct[:, blk * 16:(blk + 1) * 16],
                        axis=AX), waits=[ev_sctv, ev_s_smx])
                P.drain("vector")
                evn = P.add("vector", lambda e: e.tensor_scalar(
                    smx[:, 4:8], smx[:, 0:4], -1.0, None, op0=A.mult), inc="v")
                P.drain("vector")
                eve = None
                for blk in range(4):
                    eve = P.add("scalar", lambda e, blk=blk: e.activation(
                        wts[:, blk * KW:(blk + 1) * KW],
                        sct[:, blk * 16:(blk + 1) * 16], AF.Exp,
                        bias=smx[:, 4 + blk: 5 + blk],
                        accum_out=smx[:, 8 + blk: 9 + blk]),
                        waits=[evn], inc="s")
                ev_s_sct = eve
                ev_s_smx = eve
                P.add("vector", lambda e: e.reciprocal(
                    smx[:, 12:16], smx[:, 8:12]), waits=[eve], inc="v")
                P.drain("vector")
                ev_wt = None
                for blk in range(4):
                    ev_wt = P.add("vector", lambda e, blk=blk: e.tensor_scalar(
                        wts[:, blk * KW:(blk + 1) * KW],
                        wts[:, blk * KW:(blk + 1) * KW],
                        smx[:, 12 + blk: 13 + blk], None, op0=A.mult), inc="v")
                P.drain("vector")
                return ev_wt

            def make_tail(sp, ev_wt):
                b0 = sp * 4
                state = {"ev_v2": None, "ev_bev": [None] * 4}

                def t_ub(blk, ui):
                    def f():
                        def ub(e, blk=blk, ui=ui):
                            e.memset(uwide[ui][:], 0.0)
                            last = None
                            for k in range(KW):
                                off = k + 121
                                wcol = wts[:, blk * KW + k: blk * KW + k + 1]
                                e.drain()
                                last = e.scalar_tensor_tensor(
                                    uwide[ui][:, off: off + 128], idb[:], wcol,
                                    uwide[ui][:, off: off + 128],
                                    op0=A.mult, op1=A.add)
                            return last
                        state[f"ub{blk}"] = P.add(
                            "vector", ub, waits=[ev_wt, ev_idb, uw_free[ui]],
                            inc="v")
                        P.drain("vector")
                    return f

                def t_band_tr(blk, tau, kaps, ui):
                    def f():
                        nonlocal ev_tb_last
                        wi = blk % 2

                        def utr(e, ui=ui, kaps=kaps, tau=tau):
                            last = None
                            for ki, kp in enumerate(kaps):
                                delta = kp - tau
                                last = e.transpose(
                                    ptb[:, ki * 128:(ki + 1) * 128],
                                    uwide[ui][:, (delta + 1) * 128:
                                               (delta + 2) * 128], idb[:])
                            return last
                        evt = P.add("tensor", utr,
                                    waits=[state[f"ub{blk}"], ev_tb_last],
                                    inc="t")
                        evw = P.add("vector", lambda e, wi=wi, n=len(kaps):
                                    e.tensor_copy(
                            wtb[wi][:, 0:n * 128], ptb[:, 0:n * 128]),
                            waits=[evt, wtb_free[wi]], inc="v")
                        ev_tb_last = evw
                        state[f"wt{blk}"] = evw
                    return f

                def t_band(blk, ki, kp, tau, nkaps, ui):
                    def f():
                        wi = blk % 2

                        def bmm(e, kp=kp, ki=ki, wi=wi, first=(ki == 0),
                                last=(ki == nkaps - 1)):
                            r = None
                            for gi, (s0_, wdt) in enumerate(HG):
                                r = e.matmul(pB[gi][:, 0:wdt],
                                             wtb[wi][:, ki * 128:(ki + 1) * 128],
                                             xn[:, kp * H + s0_: kp * H + s0_ + wdt],
                                             start=first, stop=last)
                            return r
                        ev_bmm = P.add("tensor", bmm,
                                       waits=[state[f"wt{blk}"], ev_xn[kp],
                                              ev_pB_free[0], ev_pB_free[1]],
                                       inc="t")
                        state[f"bmm{blk}"] = ev_bmm
                        if ki == nkaps - 1:
                            uw_free[ui] = ev_bmm
                            wtb_free[blk % 2] = ev_bmm
                    return f

                def t_stats(blk, tau):
                    def f():
                        nonlocal ev_s_bbuf
                        ev_bmm = state[f"bmm{blk}"]
                        for gi, (s0_, wdt) in enumerate(HG):
                            P.add("vector", lambda e, gi=gi, wdt=wdt: e.bn_stats(
                                bnb[:, gi * 6:(gi + 1) * 6], pB[gi][:, 0:wdt]),
                                waits=[ev_bmm])
                            P.drain("vector")
                        P.add("vector", lambda e: e.bn_aggr(
                            agg[:], bnb[:, 0:6 * nst]))
                        P.drain("vector")
                        P.add("vector", lambda e, tau=tau: e.tensor_copy(
                            stat["m2"][:, tau:tau + 1], agg[:, 0:1]))
                        state["ev_v2"] = P.add("vector", lambda e, tau=tau:
                                               e.tensor_copy(
                            stat["v2"][:, tau:tau + 1], agg[:, 1:2]), inc="v")
                        P.drain("vector")
                        for gi, (s0_, wdt) in enumerate(HG):
                            state["ev_bev"][blk] = P.add(
                                "vector", lambda e, blk=blk, s0_=s0_, wdt=wdt,
                                gi=gi: e.tensor_copy(
                                    bbuf[:, blk * H + s0_: blk * H + s0_ + wdt],
                                    pB[gi][:, 0:wdt]),
                                waits=[ev_s_bbuf], inc="v")
                            ev_pB_free[gi] = state["ev_bev"][blk]
                        P.drain("vector")
                    return f

                def t_rstd():
                    def f():
                        state["ev_a2"] = rstd_batch(
                            "m2", "v2", "r2", "a2", b0, b0 + 4, [state["ev_v2"]])
                    return f

                def t_x2(blk, tau):
                    def f():
                        nonlocal ev_s_bbuf
                        evz = P.add("scalar", lambda e, blk=blk, tau=tau:
                                    e.activation(
                            zscr[0][:], bbuf[:, blk * H:(blk + 1) * H],
                            AF.Identity,
                            bias=stat["a2"][:, tau:tau + 1],
                            scale=stat["r2"][:, tau:tau + 1]),
                            waits=[state["ev_a2"], state["ev_bev"][blk],
                                   zscr_free[0]], inc="s")
                        ev_s_bbuf = evz
                        ev_x2[tau] = P.add("vector", lambda e, tau=tau:
                                           e.tensor_tensor(
                            x2[:, tau * H:(tau + 1) * H],
                            x2[:, tau * H:(tau + 1) * H],
                            zscr[0][:], op=A.add), waits=[evz], inc="v")
                        zscr_free[0] = ev_x2[tau]
                        P.drain("vector")
                    return f

                per = []
                for blk in range(4):
                    tau = b0 + blk
                    kaps = [kp for kp in (tau - 1, tau, tau + 1) if 0 <= kp < TT]
                    ui = blk % 2
                    mm = [t_band(blk, ki, kp, tau, len(kaps), ui)
                          for ki, kp in enumerate(kaps)]
                    per.append((t_ub(blk, ui), t_band_tr(blk, tau, kaps, ui),
                                mm, t_stats(blk, tau)))
                thunks = [per[0][0], per[0][1], per[1][0]]
                for blk in range(4):
                    thunks.extend(per[blk][2])
                    if blk + 1 < 4:
                        thunks.append(per[blk + 1][1])
                    if blk + 2 < 4:
                        thunks.append(per[blk + 2][0])
                    thunks.append(per[blk][3])
                thunks.append(t_rstd())
                for blk in range(4):
                    thunks.append(t_x2(blk, b0 + blk))
                return iter(thunks)

            def warm_burst(n, waits=()):
                def wb(e, n=n):
                    last = None
                    for _ in range(n):
                        last = e.matmul(psc[:, 0:512], idb[:], argb[0][:, 0:512],
                                        start=True, stop=True)
                    return last
                return P.add("tensor", wb, waits=list(waits), inc="t")

            P.add("vector", lambda e: e.memset(argb[0][:], 0.0))
            P.drain("vector")
            ev_warm0 = P.add("vector", lambda e: e.memset(argb[1][:], 0.0), inc="v")
            ev_sc_prev = warm_burst(24, waits=[ev_idb, ev_warm0])

            emit_windows_qproj(0)
            tail = iter(())
            for sp in range(NSP):
                ev_wt = emit_scores_softmax(sp, pump=tail)
                for th_ in tail:
                    th_()
                tail = make_tail(sp, ev_wt)
                if sp + 1 < NSP:
                    emit_windows_qproj(sp + 1, pump=tail)
            for th_ in tail:
                th_()

        bar = [("t", P.count["t"]), ("v", P.count["v"]), ("s", P.count["s"])]

        with ExitStack() as mctx:
            men = mctx.enter_context
            hpr = men(nc.sbuf_tensor("hpr", [128, IC * TH], BF16))
            yfm = men(nc.sbuf_tensor("yfm", [128, HC * TH], F32))
            wgb = [men(nc.sbuf_tensor(f"wgb{i}", [128, HC * 128], BF16)) for i in range(3)]
            wub = [men(nc.sbuf_tensor(f"wub{i}", [128, HC * 128], BF16)) for i in range(3)]
            wdb = [men(nc.sbuf_tensor(f"wdb{i}", [128, IC * 128], BF16)) for i in range(3)]
            cwb2 = [men(nc.sbuf_tensor(f"cwb{i}", [128, TH], BF16)) for i in range(2)]
            sgb = men(nc.sbuf_tensor("sgb", [128, TH], BF16))
            t1b = men(nc.sbuf_tensor("t1b", [128, TH], BF16))
            lgs = men(nc.sbuf_tensor("lgs", [16, 512], F32))
            cwl = men(nc.sbuf_tensor("cwl", [128, 4 * E], F32))
            prb = men(nc.sbuf_tensor("prb", [128, E], F32))
            m8b = men(nc.sbuf_tensor("m8b", [128, 8], F32))
            kpb = men(nc.sbuf_tensor("kpb", [128, E], F32))
            Dsc8 = men(nc.sbuf_tensor("Dsc8", [128, NTB * 128], BF16))
            ytv = [men(nc.sbuf_tensor(f"ytv{i}", [128, H], F32)) for i in range(2)]
            ytv_chain = [None, None]
            osc = [men(nc.sbuf_tensor(f"osc{i}", [128, H], F32)) for i in range(2)]
            assert TH <= 1024 and H <= 1024
            pg = men(nc.psum_tensor("pg", [128, max(TH, H)], F32))
            pu = men(nc.psum_tensor("pu", [128, max(TH, H)], F32))
            py = [men(nc.psum_tensor(f"py{i}", [128, TH], F32)) for i in range(2)]

            xm_fm = men(nc.sbuf_tensor("xm_fm", [128, HC * T], BF16))
            xmt = [men(nc.sbuf_tensor("xmt0", [128, H], BF16))] * 2
            xmt_chain = [None]
            free_ev = {"pg": None, "pu": None, "py0": None, "py1": None}
            pbank = {"pg": pg, "pu": pu, "py0": py[0], "py1": py[1]}

            ev_xm_h = [None, None]
            ev_cw_h = [None, None]
            ev_pu_free0 = None
            xmt_free = [None, None]
            ln3_state = {"v3": None, "a3": [None, None]}

            def emit_ln3_stats(lo, hi, hf):
                for tau in range(lo, hi):
                    ln3_state["v3"] = ln_stats(
                        lambda e, s0, wdt, tau=tau:
                        x2[:, tau * H + s0: tau * H + s0 + wdt],
                        "m3", "v3", tau, [ev_x2[tau]])
                ln3_state["a3"][hf] = rstd_batch(
                    "m3", "v3", "r3", "a3", lo, hi, [ln3_state["v3"]])

            def emit_ln3_tile(tau, hf):
                nonlocal ev_pu_free0
                xi = tau % 2
                evx = P.add("scalar", lambda e, tau=tau, xi=xi: e.activation(
                    xmt[xi][:], x2[:, tau * H:(tau + 1) * H], AF.Identity,
                    bias=stat["a3"][:, tau:tau + 1],
                    scale=stat["r3"][:, tau:tau + 1]),
                    waits=[ln3_state["a3"][hf], xmt_chain[0]] + bar, inc="s")

                def xtr(e, xi=xi):
                    pub = pu[:, 0:512].bitcast(BF16)
                    last = None
                    for c in range(HC):
                        last = e.transpose(
                            pub[:, c * 128:(c + 1) * 128],
                            xmt[xi][:, c * 128:(c + 1) * 128], idb[:])
                    return last
                evt = P.add("tensor", xtr,
                            waits=[evx, ev_idb, ev_pu_free0, free_ev["pu"]],
                            inc="t")
                xm3 = xm_fm.rearrange("p (c t) -> p c t", t=T)
                ev_pu_free0 = P.add("vector", lambda e, tau=tau: e.tensor_copy(
                    xm3[:, :, tau * 128:(tau + 1) * 128],
                    pu[:, 0:512].bitcast(BF16)[:, 0:HC * 128]),
                    waits=[evt] + bar, inc="v")
                free_ev["pu"] = ev_pu_free0
                xmt_chain[0] = ev_pu_free0
                ev_xm_h[hf] = ev_pu_free0

            rt_state = {"lgs_free": None, "cwt": None, "s_smx2": None,
                        "s_cwl": None}

            def emit_router_mm(g):
                def lmm(e, g=g):
                    last = None
                    for k in range(HC):
                        last = e.matmul(
                            pg[0:E, 0:512], wrt[:, k * E:(k + 1) * E],
                            xm_fm[:, k * T + g * 512: k * T + (g + 1) * 512],
                            start=(k == 0), stop=(k == HC - 1))
                    return last
                hf = (g * 512) // TH
                evl = P.add("tensor", lmm,
                            waits=[ev_wrt, ev_xm_h[hf], rt_state["lgs_free"],
                                   free_ev["pg"]], inc="t")
                evlg = P.add("vector", lambda e: e.tensor_copy(
                    lgs[0:E, :], pg[0:E, 0:512]), waits=[evl], inc="v")
                free_ev["pg"] = evlg
                P.drain("vector")

                def rtr(e):
                    last = None
                    for j in range(4):
                        last = e.transpose(pu[:, j * E:(j + 1) * E],
                                           lgs[0:E, j * 128:(j + 1) * 128],
                                           idf[0:E, 0:E])
                    return last
                evt = P.add("tensor", rtr,
                            waits=[evlg, rt_state["cwt"], free_ev["pu"]],
                            inc="t")
                rt_state["lgs_free"] = evt
                evc = P.add("vector", lambda e: e.tensor_copy(
                    cwl[:], pu[:, 0:4 * E]), waits=[evt, rt_state["s_cwl"]],
                    inc="v")
                free_ev["pu"] = evc
                rt_state["cwt"] = evc
                P.drain("vector")

            def emit_router_cw(g, j, hf):
                tau = g * 4 + j
                P.add("vector", lambda e, j=j: e.reduce_max(
                    smx[:, 0:1], cwl[:, j * E:(j + 1) * E], axis=AX),
                    waits=[rt_state["cwt"]])
                P.drain("vector")
                evn = P.add("vector", lambda e: e.tensor_scalar(
                    smx[:, 1:2], smx[:, 0:1], -1.0, None, op0=A.mult),
                    waits=[rt_state["s_smx2"]], inc="v")
                eve = P.add("scalar", lambda e, j=j: e.activation(
                    prb[:], cwl[:, j * E:(j + 1) * E], AF.Exp, bias=smx[:, 1:2],
                    accum_out=smx[:, 2:3]), waits=[evn], inc="s")
                rt_state["s_cwl"] = eve
                rt_state["s_smx2"] = eve
                P.add("vector", lambda e: e.reciprocal(
                    smx[:, 3:4], smx[:, 2:3]), waits=[eve])
                P.drain("vector")
                P.add("vector", lambda e: e.tensor_scalar(
                    prb[:], prb[:], smx[:, 3:4], None, op0=A.mult))
                P.drain("vector")
                P.add("vector", lambda e: e.max(out=m8b[:], in_=prb[:]))
                P.drain("vector")
                P.add("vector", lambda e: e.tensor_scalar(
                    kpb[:], prb[:], m8b[:, 1:2], None, op0=A.is_ge))
                P.drain("vector")
                P.add("vector", lambda e: e.tensor_tensor(
                    kpb[:], kpb[:], prb[:], op=A.mult))
                P.drain("vector")
                P.add("vector", lambda e: e.tensor_tensor(
                    smx[:, 4:5], m8b[:, 0:1], m8b[:, 1:2], op=A.add))
                P.drain("vector")
                P.add("vector", lambda e: e.tensor_scalar(
                    smx[:, 4:5], smx[:, 4:5], 1e-20, None, op0=A.add))
                P.drain("vector")
                P.add("vector", lambda e: e.reciprocal(
                    smx[:, 5:6], smx[:, 4:5]), inc="v")
                P.drain("vector")
                ev = P.add("vector", lambda e, tau=tau: e.tensor_scalar(
                    cw_all[:, tau * E:(tau + 1) * E], kpb[:],
                    smx[:, 5:6], None, op0=A.mult), inc="v")
                P.drain("vector")
                ev_cw_h[hf] = ev

            def ln3_router_thunks(hf):
                lo = hf * NTB
                th_list = []
                th_list.append(lambda: emit_ln3_stats(lo, lo + NTB, hf))
                for tau in range(lo, lo + NTB):
                    th_list.append(lambda tau=tau: emit_ln3_tile(tau, hf))
                for g in range(lo // 4, (lo + NTB) // 4):
                    th_list.append(lambda g=g: emit_router_mm(g))
                    for j in range(4):
                        th_list.append(lambda g=g, j=j: emit_router_cw(g, j, hf))
                return th_list

            NE = E + 1
            wgb_free = [None, None, None]
            wub_free = [None, None, None]
            wdb_free = [None, None, None]
            gname = ["pg", "py0"]
            uname = ["pu", "py1"]
            pyname = ["py0", "py1"]
            ev_cwb_ready = [None, None]
            ev_cwb_cons2 = [None, None]
            ev_sgb_cons = None
            ev_t1b_cons = None
            ev_dsc_cons = None
            ev_out_dma = []
            nload = 0

            def emit_cwb_d(eidx, th):
                nonlocal ev_dsc_cons
                evd = None
                for blk in range(NTB):
                    tau = th * NTB + blk
                    evd = P.add("vector", lambda e, tau=tau, eidx=eidx, blk=blk:
                                e.tensor_scalar(
                        Dsc8[:, blk * 128:(blk + 1) * 128], idb[:],
                        cw_all[:, tau * E + eidx: tau * E + eidx + 1],
                        None, op0=A.mult),
                        waits=[ev_cw_h[th], ev_idb, ev_dsc_cons],
                        inc="v" if blk == NTB - 1 else None)
                return evd

            def emit_cwb_m(eidx, bankname, evd):
                nonlocal ev_dsc_cons
                bank = pbank[bankname]
                ri = eidx % 2

                def cwmm(e, bank=bank):
                    last = None
                    for blk in range(NTB):
                        last = e.matmul(
                            bank[:, blk * 128:(blk + 1) * 128], onesb[:],
                            Dsc8[:, blk * 128:(blk + 1) * 128],
                            start=True, stop=True)
                    return last
                evm = P.add("tensor", cwmm,
                            waits=[evd, ev_onesb, free_ev[bankname]], inc="t")
                ev_dsc_cons = evm
                ev = P.add("vector", lambda e, ri=ri, bank=bank: e.tensor_copy(
                    cwb2[ri][:], bank[:, 0:NTB * 128]),
                    waits=[evm, ev_cwb_cons2[ri]], inc="v")
                free_ev[bankname] = ev
                P.drain("vector")
                ev_cwb_ready[ri] = ev

            def emit_cwb(eidx, th, bankname):
                emit_cwb_m(eidx, bankname, emit_cwb_d(eidx, th))

            for th_fn in ln3_router_thunks(0):
                th_fn()
            moe_pump = iter(ln3_router_thunks(1))

            for th in range(2):
                tof = th * TH
                ev_y = [None] * HC
                if th == 1:
                    for th_fn in moe_pump:
                        th_fn()
                emit_cwb(0, th, "pg")
                for ei in range(NE):
                    routed = ei < E
                    ri = ei % 2

                    ev_h = None
                    evd_next = [None]
                    evld_pre = {}
                    for ic in range(IC):
                        sl = nload % 3
                        gn, un = gname[ic % 2], uname[ic % 2]
                        gb, ub = pbank[gn], pbank[un]
                        if routed:
                            src_g = d_wg[ei, :, ic * 128:(ic + 1) * 128]
                            src_u = d_wu[ei, :, ic * 128:(ic + 1) * 128]
                        else:
                            src_g = d_wgs[:, ic * 128:(ic + 1) * 128]
                            src_u = d_wus[:, ic * 128:(ic + 1) * 128]
                        evlg_ = P.add("gpsimd", lambda e, sl=sl, src_g=src_g: e.dma_start(
                            out=wgb[sl][:],
                            in_=src_g.rearrange("(c p) j -> p c j", p=128)),
                            waits=[wgb_free[sl]] + bar, inc=f"dga{sl}", inc_amt=16)
                        evlu_ = P.add("gpsimd", lambda e, sl=sl, src_u=src_u: e.dma_start(
                            out=wub[sl][:],
                            in_=src_u.rearrange("(c p) j -> p c j", p=128)),
                            waits=[wub_free[sl]] + bar, inc=f"dgb{sl}", inc_amt=16)
                        nload += 1

                        def gmm(e, sl=sl, tof=tof, gb=gb):
                            last = None
                            for c in range(HC):
                                for gg in range(NTG):
                                    last = e.matmul(
                                        gb[:, gg * 512:(gg + 1) * 512],
                                        wgb[sl][:, c * 128:(c + 1) * 128],
                                        xm_fm[:, c * T + tof + gg * 512:
                                              c * T + tof + (gg + 1) * 512],
                                        start=(c == 0), stop=(c == HC - 1))
                            return last
                        evg = P.add("tensor", gmm,
                                    waits=[evlg_, ev_xm_h[th], free_ev[gn]],
                                    inc="t")
                        wgb_free[sl] = evg

                        def umm(e, sl=sl, tof=tof, ub=ub):
                            last = None
                            for c in range(HC):
                                for gg in range(NTG):
                                    last = e.matmul(
                                        ub[:, gg * 512:(gg + 1) * 512],
                                        wub[sl][:, c * 128:(c + 1) * 128],
                                        xm_fm[:, c * T + tof + gg * 512:
                                              c * T + tof + (gg + 1) * 512],
                                        start=(c == 0), stop=(c == HC - 1))
                            return last
                        evu = P.add("tensor", umm,
                                    waits=[evlu_, ev_xm_h[th], free_ev[un]],
                                    inc="t")
                        wub_free[sl] = evu
                        evsg = P.add("scalar", lambda e, gb=gb: e.activation(
                            sgb[:], gb[:, 0:TH], AF.Silu),
                            waits=[evg, ev_sgb_cons], inc="s")
                        free_ev[gn] = evsg
                        evt1 = P.add("vector", lambda e, ub=ub: e.tensor_tensor(
                            t1b[:], sgb[:], ub[:, 0:TH], op=A.mult),
                            waits=[evsg, evu, ev_t1b_cons], inc="v")
                        ev_sgb_cons = evt1
                        free_ev[un] = evt1
                        P.drain("vector")
                        if routed:
                            ev_h = P.add("vector", lambda e, ic=ic, ri=ri:
                                         e.tensor_tensor(
                                hpr[:, ic * TH:(ic + 1) * TH], t1b[:], cwb2[ri][:],
                                op=A.mult),
                                waits=[evt1, ev_cwb_ready[ri]], inc="v")
                            ev_cwb_cons2[ri] = ev_h
                        else:
                            ev_h = P.add("vector", lambda e, ic=ic: e.tensor_copy(
                                hpr[:, ic * TH:(ic + 1) * TH], t1b[:]),
                                waits=[evt1], inc="v")
                        ev_t1b_cons = ev_h
                        P.drain("vector")
                        if ic == IC - 2 and routed and ei + 1 < E:
                            evd_next[0] = emit_cwb_d(ei + 1, th)
                        if IC >= 4 and ic >= IC - 3:
                            cpre = ic - (IC - 3)
                            if cpre < HC and cpre not in evld_pre:
                                sl2 = cpre % 3
                                if routed:
                                    src_d2 = d_wd[ei, :, cpre * 128:(cpre + 1) * 128]
                                else:
                                    src_d2 = d_wds[:, cpre * 128:(cpre + 1) * 128]
                                evld_pre[cpre] = P.add(
                                    "gpsimd", lambda e, sl2=sl2, src_d2=src_d2:
                                    e.dma_start(
                                        out=wdb[sl2][:],
                                        in_=src_d2.rearrange(
                                            "(ic p) j -> p ic j", p=128)),
                                    waits=[wdb_free[sl2]] + bar,
                                    inc=f"dgd{sl2}", inc_amt=16)

                    for c in range(HC):
                        sl = c % 3
                        pyn = pyname[c % 2]
                        if routed:
                            src_d = d_wd[ei, :, c * 128:(c + 1) * 128]
                        else:
                            src_d = d_wds[:, c * 128:(c + 1) * 128]
                        if c in evld_pre:
                            evld = evld_pre[c]
                        else:
                            evld = P.add("gpsimd", lambda e, sl=sl, src_d=src_d:
                                         e.dma_start(
                                out=wdb[sl][:],
                                in_=src_d.rearrange("(ic p) j -> p ic j", p=128)),
                                waits=[wdb_free[sl]] + bar, inc=f"dgd{sl}",
                                inc_amt=16)

                        def dmm(e, sl=sl, c=c, pyn=pyn):
                            last = None
                            for ic in range(IC):
                                for gg in range(NTG):
                                    last = e.matmul(
                                        pbank[pyn][:, gg * 512:(gg + 1) * 512],
                                        wdb[sl][:, ic * 128:(ic + 1) * 128],
                                        hpr[:, ic * TH + gg * 512:
                                            ic * TH + (gg + 1) * 512],
                                        start=(ic == 0), stop=(ic == IC - 1))
                            return last
                        evdm = P.add("tensor", dmm,
                                     waits=[evld, ev_h, free_ev[pyn]], inc="t")
                        wdb_free[sl] = evdm
                        if ei == 0:
                            ev_y[c] = P.add("vector", lambda e, c=c, pyn=pyn:
                                            e.tensor_copy(
                                yfm[:, c * TH:(c + 1) * TH],
                                pbank[pyn][:, 0:TH]),
                                waits=[evdm, ev_y[c]], inc="v")
                        else:
                            ev_y[c] = P.add("vector", lambda e, c=c, pyn=pyn:
                                            e.scalar_tensor_tensor(
                                yfm[:, c * TH:(c + 1) * TH], pbank[pyn][:, 0:TH],
                                1.0, yfm[:, c * TH:(c + 1) * TH],
                                op0=A.mult, op1=A.add),
                                waits=[evdm, ev_y[c]], inc="v")
                        free_ev[pyn] = ev_y[c]
                        P.drain("vector")
                        if c == 0 and routed and ei + 1 < E:
                            if evd_next[0] is None:
                                evd_next[0] = emit_cwb_d(ei + 1, th)
                            emit_cwb_m(ei + 1, "pg", evd_next[0])
                        if th == 0 and c in (2, 4, 6):
                            th_fn = next(moe_pump, None)
                            if th_fn is not None:
                                th_fn()

                for blk in range(NTB):
                    tau = th * NTB + blk
                    obk = "pg" if blk % 2 == 0 else "pu"
                    ob = pbank[obk]
                    ev_yt = None
                    for c in range(HC):
                        ev_yt = P.add("tensor", lambda e, c=c, blk=blk, ob=ob:
                                      e.transpose(
                            ob[:, c * 128:(c + 1) * 128],
                            yfm[:, c * TH + blk * 128: c * TH + (blk + 1) * 128],
                            idf[:]),
                            waits=[ev_y[c], ev_id, free_ev[obk]], inc="t")
                    free_ev[obk] = ev_yt
                    po = 4 * (blk % 2)
                    for gi in range((H + 511) // 512):
                        s0_ = gi * 512
                        wdt = min(512, H - s0_)
                        P.add("vector", lambda e, gi=gi, s0_=s0_, wdt=wdt, ob=ob,
                              blk=blk: e.bn_stats(
                            bnb[:, (blk % 2) * 12 + gi * 6:
                                (blk % 2) * 12 + (gi + 1) * 6],
                            ob[:, s0_: s0_ + wdt]),
                            waits=[ev_yt])
                        P.drain("vector")
                    P.add("vector", lambda e, po=po, blk=blk: e.bn_aggr(
                        agg2[:, po: po + 2],
                        bnb[:, (blk % 2) * 12: (blk % 2) * 12 + 6 * nst]))
                    P.drain("vector")
                    evv = P.add("vector", lambda e, po=po: e.tensor_scalar(
                        agg2[:, po + 2: po + 3], agg2[:, po + 1: po + 2], EPS,
                        None, op0=A.add), inc="v")
                    P.drain("vector")
                    evq = P.add("scalar", lambda e, po=po: e.activation(
                        agg2[:, po + 2: po + 3], agg2[:, po + 2: po + 3],
                        AF.Sqrt), waits=[evv], inc="s")
                    ev_a4 = P.add("vector", lambda e, po=po: e.reciprocal(
                        agg2[:, po + 3: po + 4], agg2[:, po + 2: po + 3]),
                        waits=[evq], inc="v")
                    P.drain("vector")
                    zi = blk % 2
                    evz = P.add("vector", lambda e, zi=zi, po=po, ob=ob:
                                e.tensor_scalar(
                        ytv[zi][:], ob[:, 0:H], agg2[:, po: po + 1],
                        agg2[:, po + 3: po + 4], op0=A.subtract, op1=A.mult),
                        waits=[ev_a4, ytv_chain[zi]], inc="v")
                    free_ev[obk] = evz
                    P.drain("vector")
                    evo = P.add("vector", lambda e, zi=zi, tau=tau: e.tensor_tensor(
                        osc[zi][:], ytv[zi][:], x2[:, tau * H:(tau + 1) * H],
                        op=A.add), waits=[evz] + ev_out_dma[-2:-1], inc="v")
                    ytv_chain[zi] = evo
                    P.drain("vector")
                    ev_out_dma.append(P.add("sync", lambda e, zi=zi, tau=tau: e.dma_start(
                        out=d_out.rearrange("(tt p) h -> p tt h", p=128)[
                            :, tau, :],
                        in_=osc[zi][:]), waits=[evo], inc=f"do{zi}", inc_amt=16))

            P.add("sync", None, waits=[("do0", P.count["do0"]),
                                       ("do1", P.count["do1"])])

        P.replay(en(nc.Block()))
    return nc


_NC_CACHE = {}


def _get_nc(key, cfg):
    if key not in _NC_CACHE:
        _NC_CACHE[key] = build_program(cfg)
    return _NC_CACHE[key]


def make_inputs(cfg, x_b, w1, w2, v, w_router, wg, wu, wd, wgs, wus, wds):
    H, E = cfg.H, cfg.E
    vsp = np.zeros((H, KW * KW), dtype=np.float32)
    for k in range(KW):
        vsp[:, k * KW + k] = v[:, 0]
    iw = np.zeros((128, 448), dtype=np.float32)
    for p in range(128):
        iw[p, p + 160] = 1.0
    ident = np.eye(128, dtype=np.float32)
    ones = np.ones((128, 128), dtype=np.float32)
    return {
        "x": np.ascontiguousarray(x_b),
        "w1": np.ascontiguousarray(w1),
        "w2": np.ascontiguousarray(w2),
        "vsp": vsp,
        "wrt": np.ascontiguousarray(w_router.T),
        "wg": np.ascontiguousarray(wg),
        "wu": np.ascontiguousarray(wu),
        "wd": np.ascontiguousarray(wd),
        "wgs": np.ascontiguousarray(wgs),
        "wus": np.ascontiguousarray(wus),
        "wds": np.ascontiguousarray(wds),
        "iw": iw,
        "ident": ident,
        "ones": ones,
    }


def kernel(L0_attn_inputs, L0_attn_w1, L0_attn_w2, L0_attn_v,
           L0_ffn_w_router, L0_ffn_wg, L0_ffn_wu, L0_ffn_wd,
           L0_ffn_wg_s, L0_ffn_wu_s, L0_ffn_wd_s):
    x = np.asarray(L0_attn_inputs, dtype=np.float32)
    B, T, H = x.shape
    cfg = Cfg(T=T, H=H, I=np.asarray(L0_ffn_wg).shape[2],
              E=np.asarray(L0_ffn_w_router).shape[0])
    nc = _get_nc((T, H, cfg.I, cfg.E), cfg)
    common = dict(
        w1=np.asarray(L0_attn_w1, np.float32), w2=np.asarray(L0_attn_w2, np.float32),
        v=np.asarray(L0_attn_v, np.float32),
        w_router=np.asarray(L0_ffn_w_router, np.float32),
        wg=np.asarray(L0_ffn_wg, np.float32), wu=np.asarray(L0_ffn_wu, np.float32),
        wd=np.asarray(L0_ffn_wd, np.float32),
        wgs=np.asarray(L0_ffn_wg_s, np.float32),
        wus=np.asarray(L0_ffn_wu_s, np.float32),
        wds=np.asarray(L0_ffn_wd_s, np.float32))
    in_maps = [make_inputs(cfg, x[b], **common) for b in range(B)]
    res = run_bass_kernel_spmd(nc, in_maps, core_ids=list(range(B)))
    out = np.stack([np.asarray(res.results[b]["out"]) for b in range(B)], axis=0)
    return out.astype(np.float32)


# version 35 (final)
# speedup vs baseline: 1.0106x; 1.0056x over previous
import numpy as np
import ml_dtypes
from contextlib import ExitStack

import concourse.bass as bass
import concourse.mybir as mybir
from concourse.alu_op_type import AluOpType as A
from concourse.bass_utils import run_bass_kernel_spmd

F32 = mybir.dt.float32
BF16 = mybir.dt.bfloat16
AX = mybir.AxisListType.X
AF = mybir.ActivationFunctionType

N_CORES = 8
EPS = 1e-5
KW = 16


class Cfg:
    def __init__(self, T=2048, H=1024, I=1408, E=8):
        assert T % 512 == 0 and H % 128 == 0 and I % 128 == 0
        self.T, self.H, self.I, self.E = T, H, I, E
        self.TT = T // 128
        self.HC = H // 128
        self.IC = I // 128
        self.NSP = T // 512
        self.TH = T // 2
        self.NTB = self.TH // 128
        self.NTG = self.TH // 512
        self.HG = [(s, min(512, H - s)) for s in range(0, H, 512)]
        self.nst = len(self.HG)


class Prog:
    ENGINES = ("sync", "gpsimd", "tensor", "vector", "scalar")

    def __init__(self, sems):
        self.sems = sems
        self.count = {k: 0 for k in sems}
        self.ops = {e: [] for e in self.ENGINES}
        self.hwm = {e: {} for e in self.ENGINES}

    def add(self, engine, fn, waits=(), inc=None, inc_amt=1):
        wl = []
        for w in waits:
            if w is None:
                continue
            semname, val = w
            if val <= 0 or self.hwm[engine].get(semname, -1) >= val:
                continue
            self.hwm[engine][semname] = val
            wl.append((semname, val))
        ev = None
        if inc is not None:
            self.count[inc] += inc_amt
            ev = (inc, self.count[inc])
        self.ops[engine].append((fn, wl, (inc, inc_amt) if inc else None))
        return ev

    def drain(self, engine):
        self.ops[engine].append((lambda e: e.drain(), [], None))

    def replay(self, block):
        sems, ops = self.sems, self.ops

        def mk(engname):
            def body(eng):
                for fn, wl, inc in ops[engname]:
                    for semname, val in wl:
                        eng.wait_ge(sems[semname], val)
                    if fn is None:
                        continue
                    inst = fn(eng)
                    if inc is not None:
                        inst.then_inc(sems[inc[0]], inc[1])
            body.__name__ = "_"
            return body

        for engname in self.ENGINES:
            getattr(block, engname)(mk(engname))


def build_program(cfg: Cfg):
    nc = bass.Bass("TRN2", target_bir_lowering=False)
    T, H, I, E = cfg.T, cfg.H, cfg.I, cfg.E
    TT, HC, IC, NSP, TH = cfg.TT, cfg.HC, cfg.IC, cfg.NSP, cfg.TH
    NTB, NTG, HG, nst = cfg.NTB, cfg.NTG, cfg.HG, cfg.nst

    d_x = nc.dram_tensor("x", [T, H], F32, kind="ExternalInput").ap()
    d_w1 = nc.dram_tensor("w1", [H, H], F32, kind="ExternalInput").ap()
    d_w2 = nc.dram_tensor("w2", [H, H], F32, kind="ExternalInput").ap()
    d_vsp = nc.dram_tensor("vsp", [H, KW * KW], F32, kind="ExternalInput").ap()
    d_wrt = nc.dram_tensor("wrt", [H, E], F32, kind="ExternalInput").ap()
    d_wg = nc.dram_tensor("wg", [E, H, I], F32, kind="ExternalInput").ap()
    d_wu = nc.dram_tensor("wu", [E, H, I], F32, kind="ExternalInput").ap()
    d_wd = nc.dram_tensor("wd", [E, I, H], F32, kind="ExternalInput").ap()
    d_wgs = nc.dram_tensor("wgs", [H, I], F32, kind="ExternalInput").ap()
    d_wus = nc.dram_tensor("wus", [H, I], F32, kind="ExternalInput").ap()
    d_wds = nc.dram_tensor("wds", [I, H], F32, kind="ExternalInput").ap()
    d_iw = nc.dram_tensor("iw", [128, 448], F32, kind="ExternalInput").ap()
    d_id = nc.dram_tensor("ident", [128, 128], F32, kind="ExternalInput").ap()
    d_ones = nc.dram_tensor("ones", [128, 128], F32, kind="ExternalInput").ap()
    d_out = nc.dram_tensor("out", [T, H], F32, kind="ExternalOutput").ap()

    with ExitStack() as ctx:
        en = ctx.enter_context
        sems = {n: en(nc.semaphore(n)) for n in
                ["dx0", "dx1", "dx2", "dx3", "dc", "dgc", "dw", "dga0", "dga1", "dga2",
                 "dgb0", "dgb1", "dgb2", "dgd0", "dgd1", "dgd2", "do0", "do1",
                 "t", "v", "s", "g"]}
        P = Prog(sems)

        x2 = en(nc.sbuf_tensor("x2", [128, TT * H], F32))
        idf = en(nc.sbuf_tensor("idf", [128, 128], F32))
        idb = en(nc.sbuf_tensor("idb", [128, 128], BF16))
        onesb = en(nc.sbuf_tensor("onesb", [128, 128], BF16))
        vsp = en(nc.sbuf_tensor("vspb", [128, HC * KW * KW], BF16))
        wrt = en(nc.sbuf_tensor("wrtb", [128, HC * E], BF16))
        stat = {k: en(nc.sbuf_tensor("st_" + k, [128, TT], F32))
                for k in ["m1", "v1", "r1", "a1", "m2", "v2", "r2", "a2",
                          "m3", "v3", "r3", "a3", "m4", "v4", "r4", "a4"]}
        cw_all = en(nc.sbuf_tensor("cw_all", [128, TT * E], F32))
        bnb = en(nc.sbuf_tensor("bnb", [128, 24], F32))
        agg = en(nc.sbuf_tensor("agg", [128, 2], F32))
        agg2 = en(nc.sbuf_tensor("agg2", [128, 8], F32))
        smx = en(nc.sbuf_tensor("smx", [128, 32], F32))

        ev_xq = []
        for q in range(4):
            lo = q * (TT // 4)
            hi = (q + 1) * (TT // 4)
            ev_xq.append(P.add("sync", lambda e, lo=lo, hi=hi: e.dma_start(
                out=x2[:, lo * H: hi * H],
                in_=d_x.rearrange("(tt p) h -> p tt h", p=128)[:, lo:hi, :]),
                inc=f"dx{q}", inc_amt=16))
        ev_x = ev_xq[-1]
        ev_id = P.add("sync", lambda e: e.dma_start(out=idf[:], in_=d_id),
                      inc="dc", inc_amt=16)
        P.add("gpsimd", lambda e: e.dma_start(out=idb[:], in_=d_id),
              inc="dgc", inc_amt=16)
        P.add("gpsimd", lambda e: e.dma_start(out=onesb[:], in_=d_ones),
              inc="dgc", inc_amt=16)
        P.add("gpsimd", lambda e: e.dma_start(
            out=vsp[:], in_=d_vsp.rearrange("(c p) j -> p c j", p=128)),
            inc="dgc", inc_amt=16)
        ev_const = P.add("gpsimd", lambda e: e.dma_start(
            out=wrt[:], in_=d_wrt.rearrange("(c p) j -> p c j", p=128)),
            inc="dgc", inc_amt=16)
        ev_iw = ev_idb = ev_onesb = ev_vsp = ev_wrt = ev_const

        def ln_stats(src_fn, mkey, vkey, tau, waits):
            for ci, (s0, wdt) in enumerate(HG):
                P.add("vector", lambda e, ci=ci, s0=s0, wdt=wdt: e.bn_stats(
                    bnb[:, ci * 6:(ci + 1) * 6], src_fn(e, s0, wdt)), waits=waits)
                P.drain("vector")
            P.add("vector", lambda e: e.bn_aggr(agg[:], bnb[:, 0:6 * nst]))
            P.drain("vector")
            P.add("vector", lambda e, tau=tau: e.tensor_copy(
                stat[mkey][:, tau:tau + 1], agg[:, 0:1]))
            ev = P.add("vector", lambda e, tau=tau: e.tensor_copy(
                stat[vkey][:, tau:tau + 1], agg[:, 1:2]), inc="v")
            P.drain("vector")
            return ev

        def rstd_batch(mkey, vkey, rkey, akey, lo, hi, waits):
            eve_ = P.add("vector", lambda e: e.tensor_scalar(
                stat[rkey][:, lo:hi], stat[vkey][:, lo:hi], EPS, None, op0=A.add),
                waits=waits, inc="v")
            evq = P.add("scalar", lambda e: e.activation(
                stat[rkey][:, lo:hi], stat[rkey][:, lo:hi], AF.Sqrt),
                waits=[eve_], inc="s")
            P.add("vector", lambda e: e.reciprocal(
                stat[rkey][:, lo:hi], stat[rkey][:, lo:hi]), waits=[evq])
            P.drain("vector")
            P.add("vector", lambda e: e.tensor_scalar(
                stat[akey][:, lo:hi], stat[mkey][:, lo:hi], -1.0, None, op0=A.mult))
            P.drain("vector")
            ev = P.add("vector", lambda e: e.tensor_tensor(
                stat[akey][:, lo:hi], stat[akey][:, lo:hi], stat[rkey][:, lo:hi],
                op=A.mult), inc="v")
            P.drain("vector")
            return ev

        with ExitStack() as actx:
            aen = actx.enter_context
            xn = aen(nc.sbuf_tensor("xn", [128, TT * H], BF16))
            w1s = aen(nc.sbuf_tensor("w1s", [128, HC * H], BF16))
            w2s = aen(nc.sbuf_tensor("w2s", [128, HC * H], BF16))
            xnw = aen(nc.sbuf_tensor("xnw", [128, HC * 768], BF16))
            qw = aen(nc.sbuf_tensor("qw", [128, HC * 512], BF16))
            pjw = aen(nc.sbuf_tensor("pjw", [128, HC * 768], BF16))
            GW = min(4, HC)
            argb = [aen(nc.sbuf_tensor(f"argb{i}", [128, GW * 512], BF16)) for i in range(2)]
            tanb = [aen(nc.sbuf_tensor(f"tanb{i}", [128, GW * 512], BF16)) for i in range(2)]
            scf = aen(nc.sbuf_tensor("scf", [16, 512], F32))
            sct = aen(nc.sbuf_tensor("sct", [128, 64], F32))
            wts = aen(nc.sbuf_tensor("wts", [128, 4 * KW], F32))
            uwide = [aen(nc.sbuf_tensor(f"uwide{i}", [128, 384], BF16)) for i in range(2)]
            wtb = [aen(nc.sbuf_tensor("wtb0", [128, 384], BF16))] * 2
            bbuf = aen(nc.sbuf_tensor("bbuf", [128, 4 * H], F32))
            zscr = [aen(nc.sbuf_tensor("zscr0", [128, H], F32))] * 2
            pq = [aen(nc.psum_tensor(f"pq{i}", [128, 512], F32)) for i in range(2)]
            psc = aen(nc.psum_tensor("psc", [128, 512], F32))
            ptb = aen(nc.psum_tensor("ptb", [128, 1024], BF16))
            ptw = aen(nc.psum_tensor("ptw", [128, 1024], BF16))
            ptf = aen(nc.psum_tensor("ptf", [128, 192], F32))
            pB = [aen(nc.psum_tensor(f"pB{i}", [128, 512], F32)) for i in range(2)]

            P.add("gpsimd", lambda e: e.dma_start(
                out=w1s[:], in_=d_w1.rearrange("(c p) j -> p c j", p=128)),
                inc="dw", inc_amt=16)
            ev_w2 = P.add("gpsimd", lambda e: e.dma_start(
                out=w2s[:], in_=d_w2.rearrange("(c p) j -> p c j", p=128)),
                inc="dw", inc_amt=16)
            ev_w1 = ev_w2

            ev_v1 = None
            for tau in range(TT):
                ev_v1 = ln_stats(
                    lambda e, s0, wdt, tau=tau: x2[:, tau * H + s0: tau * H + s0 + wdt],
                    "m1", "v1", tau, [ev_xq[tau * 4 // TT]])
            ev_a1 = rstd_batch("m1", "v1", "r1", "a1", 0, TT, [ev_v1])
            ev_xn = [None] * TT
            for tau in range(TT):
                ev_xn[tau] = P.add("scalar", lambda e, tau=tau: e.activation(
                    xn[:, tau * H:(tau + 1) * H], x2[:, tau * H:(tau + 1) * H],
                    AF.Identity, bias=stat["a1"][:, tau:tau + 1],
                    scale=stat["r1"][:, tau:tau + 1]),
                    waits=[ev_a1], inc="s")

            ev_x2 = [None] * TT
            ev_tb_last = None
            ev_tw_last = None
            ev_tf_last = None
            ev_ptf_tail = None
            ev_qp_bankfree = [None, None]
            ev_pB_free = [None, None]
            ev_sc_prev = None
            argb_free = [None, None]
            tanb_free = [None, None]
            uw_free = [None, None]
            wtb_free = [None, None]
            prev_win_readers = []
            ev_s_sct = None
            ev_s_smx = None
            ev_s_bbuf = None
            zscr_free = [None]
            ev_q = ev_pj = ev_pjs = ev_last_add = None
            ev_sctv = None

            def emit_windows_qproj(sp, pump=None):
                nonlocal ev_tw_last, ev_q, ev_pj, ev_pjs, prev_win_readers, ev_ptf_tail
                b0 = sp * 4

                def pp(n=1):
                    if pump is not None:
                        for _ in range(n):
                            th_ = next(pump, None)
                            if th_ is not None:
                                th_()
                P.drain("vector")
                ev_roll = None
                if sp > 0:
                    for c in range(HC):
                        P.add("vector", lambda e, c=c: e.tensor_copy(
                            xnw[:, c * 768: c * 768 + 128],
                            xnw[:, c * 768 + 512: c * 768 + 640]),
                            waits=prev_win_readers)
                        ev_roll = P.add("vector", lambda e, c=c: e.tensor_copy(
                            xnw[:, c * 768 + 128: c * 768 + 256],
                            xnw[:, c * 768 + 640: c * 768 + 768]), inc="v")
                        ev_roll = P.add("vector", lambda e, c=c: e.tensor_copy(
                            pjw[:, c * 768: c * 768 + 128],
                            pjw[:, c * 768 + 512: c * 768 + 640]), inc="v")
                    P.drain("vector")
                    new_xn_slots = range(2, 6)
                else:
                    for c in range(HC):
                        P.add("vector", lambda e, c=c: e.memset(
                            xnw[:, c * 768: c * 768 + 256], 0.0))
                        ev_roll = P.add("vector", lambda e, c=c: e.memset(
                            pjw[:, c * 768: c * 768 + 128], 0.0), inc="v")
                    P.drain("vector")
                    new_xn_slots = range(1, 6)
                xnw3 = xnw.rearrange("p (c w) -> p c w", w=768)
                for s in new_xn_slots:
                    beta = b0 - 1 + s
                    if beta >= TT:
                        for c in range(HC):
                            ev_roll = P.add("vector", lambda e, c=c, s=s: e.memset(
                                xnw[:, c * 768 + s * 128: c * 768 + (s + 1) * 128],
                                0.0), inc="v")
                        P.drain("vector")
                        continue

                    def wtr(e, beta=beta):
                        last = None
                        for c in range(HC):
                            last = e.transpose(
                                ptw[:, c * 128:(c + 1) * 128],
                                xn[:, beta * H + c * 128: beta * H + (c + 1) * 128],
                                idb[:])
                        return last
                    evt = P.add("tensor", wtr,
                                waits=[ev_xn[beta], ev_idb, ev_tw_last, ev_roll],
                                inc="t")
                    ev_tw_last = P.add("scalar", lambda e, s=s: e.activation(
                        xnw3[:, :, s * 128:(s + 1) * 128],
                        ptw[:, 0:HC * 128], AF.Copy), waits=[evt], inc="s")
                    pp(1)
                ev_win = ev_tw_last
                ev_slot_ms = ev_roll
                win_readers = []
                for c in range(HC):
                    bk = c % 2

                    def qmm(e, c=c, bk=bk):
                        last = None
                        for k in range(HC):
                            last = e.matmul(
                                pq[bk][:, 0:512],
                                w1s[:, k * H + c * 128: k * H + (c + 1) * 128],
                                xnw[:, k * 768 + 128: k * 768 + 640],
                                start=(k == 0), stop=(k == HC - 1))
                        return last
                    evm = P.add("tensor", qmm,
                                waits=[ev_w1, ev_win, ev_slot_ms,
                                       ev_qp_bankfree[bk]], inc="t")
                    win_readers.append(evm)
                    ev_q = P.add("scalar", lambda e, c=c, bk=bk: e.activation(
                        qw[:, c * 512:(c + 1) * 512], pq[bk][:, 0:512],
                        AF.Copy),
                        waits=[evm, ev_last_add], inc="s")
                    ev_qp_bankfree[bk] = ev_q
                    pp()
                for c in range(HC):
                    bk = c % 2

                    def pmm(e, c=c, bk=bk):
                        for k in range(HC):
                            e.matmul(
                                pq[bk][:, 0:512],
                                w2s[:, k * H + c * 128: k * H + (c + 1) * 128],
                                xnw[:, k * 768 + 128: k * 768 + 640],
                                start=(k == 0), stop=(k == HC - 1))
                        last = None
                        for k in range(HC):
                            last = e.matmul(
                                ptf[:, 64:192],
                                w2s[:, k * H + c * 128: k * H + (c + 1) * 128],
                                xnw[:, k * 768 + 640: k * 768 + 768],
                                start=(k == 0), stop=(k == HC - 1))
                        return last
                    evm = P.add("tensor", pmm,
                                waits=[ev_w2, ev_win, ev_slot_ms,
                                       ev_qp_bankfree[bk],
                                       ev_ptf_tail, ev_sctv], inc="t")
                    win_readers.append(evm)
                    evpjs = P.add("scalar", lambda e, c=c, bk=bk: e.activation(
                        pjw[:, c * 768 + 128: c * 768 + 640], pq[bk][:, 0:512],
                        AF.Copy),
                        waits=[evm, ev_last_add, ev_roll], inc="s")
                    win_readers.append(evpjs)
                    ev_pjs = evpjs
                    ev_pj = P.add("scalar", lambda e, c=c: e.activation(
                        pjw[:, c * 768 + 640: c * 768 + 768], ptf[:, 64:192],
                        AF.Copy),
                        waits=[evm, ev_last_add], inc="s")
                    ev_qp_bankfree[bk] = evpjs
                    ev_ptf_tail = ev_pj
                    pp()
                prev_win_readers = win_readers
                P.drain("vector")

            def emit_scores_softmax(sp, pump=None):
                nonlocal ev_sc_prev, ev_s_sct, ev_s_smx, ev_sctv, ev_tf_last
                nonlocal ev_last_add
                ev_q_sp, ev_pj_sp, ev_pjs_sp = ev_q, ev_pjs, ev_pj

                def spp(n=1):
                    if pump is not None:
                        for _ in range(n):
                            th_ = next(pump, None)
                            if th_ is not None:
                                th_()
                ev_sc = None
                nq = 0
                NQ = KW * (HC // GW)
                for k in range(KW):
                    for cq in range(HC // GW):
                        i = nq % 2
                        eva = None
                        for ci in range(GW):
                            c = cq * GW + ci
                            eva = P.add("vector", lambda e, k=k, c=c, ci=ci, i=i:
                                        e.tensor_tensor(
                                argb[i][:, ci * 512:(ci + 1) * 512],
                                pjw[:, c * 768 + 121 + k: c * 768 + 121 + k + 512],
                                qw[:, c * 512:(c + 1) * 512], op=A.add),
                                waits=[ev_q_sp, ev_pj_sp, ev_pjs_sp, argb_free[i]],
                                inc="v" if ci == GW - 1 else None)
                        evth = P.add("scalar", lambda e, i=i: e.activation(
                            tanb[i][:], argb[i][:], AF.Tanh),
                            waits=[eva, tanb_free[i]], inc="s")
                        argb_free[i] = evth
                        for ci in range(GW):
                            c = cq * GW + ci
                            ev_sc = P.add("tensor", lambda e, k=k, c=c, ci=ci, i=i,
                                          first=(nq == 0 and ci == 0),
                                          last=(nq == NQ - 1 and ci == GW - 1):
                                          e.matmul(
                                psc[0:16, 0:512],
                                vsp[:, c * KW * KW + k * KW:
                                    c * KW * KW + (k + 1) * KW],
                                tanb[i][:, ci * 512:(ci + 1) * 512],
                                start=first, stop=last),
                                waits=[evth, ev_vsp, ev_sc_prev],
                                inc="t" if ci == GW - 1 else None)
                        tanb_free[i] = ev_sc
                        nq += 1
                        spp(1)
                ev_last_add = eva
                ev_scev = P.add("vector", lambda e: e.tensor_copy(
                    scf[:], psc[0:16, 0:512]), waits=[ev_sc], inc="v")
                ev_sc_prev = ev_scev
                ev_sct = None
                for blk in range(4):
                    ev_sct = P.add("tensor", lambda e, blk=blk: e.transpose(
                        ptf[:, blk * 16:(blk + 1) * 16],
                        scf[:, blk * 128:(blk + 1) * 128], idf[0:16, 0:16]),
                        waits=[ev_scev, ev_id, ev_tf_last, ev_ptf_tail], inc="t")
                ev_sctv = P.add("vector", lambda e: e.tensor_copy(
                    sct[:], ptf[:, 0:64]), waits=[ev_sct, ev_s_sct], inc="v")
                ev_tf_last = ev_sctv
                P.drain("vector")
                for blk in range(4):
                    P.add("vector", lambda e, blk=blk: e.reduce_max(
                        smx[:, blk: blk + 1], sct[:, blk * 16:(blk + 1) * 16],
                        axis=AX), waits=[ev_sctv, ev_s_smx])
                P.drain("vector")
                evn = P.add("vector", lambda e: e.tensor_scalar(
                    smx[:, 4:8], smx[:, 0:4], -1.0, None, op0=A.mult), inc="v")
                P.drain("vector")
                eve = None
                for blk in range(4):
                    eve = P.add("scalar", lambda e, blk=blk: e.activation(
                        wts[:, blk * KW:(blk + 1) * KW],
                        sct[:, blk * 16:(blk + 1) * 16], AF.Exp,
                        bias=smx[:, 4 + blk: 5 + blk],
                        accum_out=smx[:, 8 + blk: 9 + blk]),
                        waits=[evn], inc="s")
                ev_s_sct = eve
                ev_s_smx = eve
                P.add("vector", lambda e: e.reciprocal(
                    smx[:, 12:16], smx[:, 8:12]), waits=[eve], inc="v")
                P.drain("vector")
                ev_wt = None
                for blk in range(4):
                    ev_wt = P.add("vector", lambda e, blk=blk: e.tensor_scalar(
                        wts[:, blk * KW:(blk + 1) * KW],
                        wts[:, blk * KW:(blk + 1) * KW],
                        smx[:, 12 + blk: 13 + blk], None, op0=A.mult), inc="v")
                P.drain("vector")
                return ev_wt

            def make_tail(sp, ev_wt):
                b0 = sp * 4
                state = {"ev_v2": None, "ev_bev": [None] * 4}

                def t_ub(blk, ui):
                    def f():
                        def ub(e, blk=blk, ui=ui):
                            e.memset(uwide[ui][:], 0.0)
                            last = None
                            for k in range(KW):
                                off = k + 121
                                wcol = wts[:, blk * KW + k: blk * KW + k + 1]
                                e.drain()
                                last = e.scalar_tensor_tensor(
                                    uwide[ui][:, off: off + 128], idb[:], wcol,
                                    uwide[ui][:, off: off + 128],
                                    op0=A.mult, op1=A.add)
                            return last
                        state[f"ub{blk}"] = P.add(
                            "vector", ub, waits=[ev_wt, ev_idb, uw_free[ui]],
                            inc="v")
                        P.drain("vector")
                    return f

                def t_band_tr(blk, tau, kaps, ui):
                    def f():
                        nonlocal ev_tb_last
                        wi = blk % 2

                        def utr(e, ui=ui, kaps=kaps, tau=tau):
                            last = None
                            for ki, kp in enumerate(kaps):
                                delta = kp - tau
                                last = e.transpose(
                                    ptb[:, ki * 128:(ki + 1) * 128],
                                    uwide[ui][:, (delta + 1) * 128:
                                               (delta + 2) * 128], idb[:])
                            return last
                        evt = P.add("tensor", utr,
                                    waits=[state[f"ub{blk}"], ev_tb_last],
                                    inc="t")
                        evw = P.add("scalar", lambda e, wi=wi, n=len(kaps):
                                    e.activation(
                            wtb[wi][:, 0:n * 128], ptb[:, 0:n * 128], AF.Copy),
                            waits=[evt, wtb_free[wi]], inc="s")
                        ev_tb_last = evw
                        state[f"wt{blk}"] = evw
                    return f

                def t_band(blk, ki, kp, tau, nkaps, ui):
                    def f():
                        wi = blk % 2

                        def bmm(e, kp=kp, ki=ki, wi=wi, first=(ki == 0),
                                last=(ki == nkaps - 1)):
                            r = None
                            for gi, (s0_, wdt) in enumerate(HG):
                                r = e.matmul(pB[gi][:, 0:wdt],
                                             wtb[wi][:, ki * 128:(ki + 1) * 128],
                                             xn[:, kp * H + s0_: kp * H + s0_ + wdt],
                                             start=first, stop=last)
                            return r
                        ev_bmm = P.add("tensor", bmm,
                                       waits=[state[f"wt{blk}"], ev_xn[kp],
                                              ev_pB_free[0], ev_pB_free[1]],
                                       inc="t")
                        state[f"bmm{blk}"] = ev_bmm
                        if ki == nkaps - 1:
                            uw_free[ui] = ev_bmm
                            wtb_free[blk % 2] = ev_bmm
                    return f

                def t_stats(blk, tau):
                    def f():
                        nonlocal ev_s_bbuf
                        ev_bmm = state[f"bmm{blk}"]
                        for gi, (s0_, wdt) in enumerate(HG):
                            P.add("vector", lambda e, gi=gi, wdt=wdt: e.bn_stats(
                                bnb[:, gi * 6:(gi + 1) * 6], pB[gi][:, 0:wdt]),
                                waits=[ev_bmm])
                            P.drain("vector")
                        P.add("vector", lambda e: e.bn_aggr(
                            agg[:], bnb[:, 0:6 * nst]))
                        P.drain("vector")
                        P.add("vector", lambda e, tau=tau: e.tensor_copy(
                            stat["m2"][:, tau:tau + 1], agg[:, 0:1]))
                        state["ev_v2"] = P.add("vector", lambda e, tau=tau:
                                               e.tensor_copy(
                            stat["v2"][:, tau:tau + 1], agg[:, 1:2]), inc="v")
                        P.drain("vector")
                        for gi, (s0_, wdt) in enumerate(HG):
                            state["ev_bev"][blk] = P.add(
                                "vector", lambda e, blk=blk, s0_=s0_, wdt=wdt,
                                gi=gi: e.tensor_copy(
                                    bbuf[:, blk * H + s0_: blk * H + s0_ + wdt],
                                    pB[gi][:, 0:wdt]),
                                waits=[ev_s_bbuf], inc="v")
                            ev_pB_free[gi] = state["ev_bev"][blk]
                        P.drain("vector")
                    return f

                def t_rstd():
                    def f():
                        state["ev_a2"] = rstd_batch(
                            "m2", "v2", "r2", "a2", b0, b0 + 4, [state["ev_v2"]])
                    return f

                def t_x2(blk, tau):
                    def f():
                        nonlocal ev_s_bbuf
                        evz = P.add("scalar", lambda e, blk=blk, tau=tau:
                                    e.activation(
                            zscr[0][:], bbuf[:, blk * H:(blk + 1) * H],
                            AF.Identity,
                            bias=stat["a2"][:, tau:tau + 1],
                            scale=stat["r2"][:, tau:tau + 1]),
                            waits=[state["ev_a2"], state["ev_bev"][blk],
                                   zscr_free[0]], inc="s")
                        ev_s_bbuf = evz
                        ev_x2[tau] = P.add("vector", lambda e, tau=tau:
                                           e.tensor_tensor(
                            x2[:, tau * H:(tau + 1) * H],
                            x2[:, tau * H:(tau + 1) * H],
                            zscr[0][:], op=A.add), waits=[evz], inc="v")
                        zscr_free[0] = ev_x2[tau]
                        P.drain("vector")
                    return f

                per = []
                for blk in range(4):
                    tau = b0 + blk
                    kaps = [kp for kp in (tau - 1, tau, tau + 1) if 0 <= kp < TT]
                    ui = blk % 2
                    mm = [t_band(blk, ki, kp, tau, len(kaps), ui)
                          for ki, kp in enumerate(kaps)]
                    per.append((t_ub(blk, ui), t_band_tr(blk, tau, kaps, ui),
                                mm, t_stats(blk, tau)))
                thunks = [per[0][0], per[0][1], per[1][0]]
                for blk in range(4):
                    thunks.extend(per[blk][2])
                    if blk + 1 < 4:
                        thunks.append(per[blk + 1][1])
                    if blk + 2 < 4:
                        thunks.append(per[blk + 2][0])
                    thunks.append(per[blk][3])
                thunks.append(t_rstd())
                for blk in range(4):
                    thunks.append(t_x2(blk, b0 + blk))
                return iter(thunks)

            def warm_burst(n, waits=()):
                def wb(e, n=n):
                    last = None
                    for _ in range(n):
                        last = e.matmul(psc[:, 0:512], idb[:], argb[0][:, 0:512],
                                        start=True, stop=True)
                    return last
                return P.add("tensor", wb, waits=list(waits), inc="t")

            P.add("vector", lambda e: e.memset(argb[0][:], 0.0))
            P.drain("vector")
            ev_warm0 = P.add("vector", lambda e: e.memset(argb[1][:], 0.0), inc="v")
            ev_sc_prev = warm_burst(24, waits=[ev_idb, ev_warm0])

            emit_windows_qproj(0)
            tail = iter(())
            for sp in range(NSP):
                ev_wt = emit_scores_softmax(sp, pump=tail)
                for th_ in tail:
                    th_()
                tail = make_tail(sp, ev_wt)
                if sp + 1 < NSP:
                    emit_windows_qproj(sp + 1, pump=tail)
            for th_ in tail:
                th_()

        bar = [("t", P.count["t"]), ("v", P.count["v"]), ("s", P.count["s"])]

        with ExitStack() as mctx:
            men = mctx.enter_context
            hpr = men(nc.sbuf_tensor("hpr", [128, IC * TH], BF16))
            yfm = men(nc.sbuf_tensor("yfm", [128, HC * TH], F32))
            wgb = [men(nc.sbuf_tensor(f"wgb{i}", [128, HC * 128], BF16)) for i in range(3)]
            wub = [men(nc.sbuf_tensor(f"wub{i}", [128, HC * 128], BF16)) for i in range(3)]
            wdb = [men(nc.sbuf_tensor(f"wdb{i}", [128, IC * 128], BF16)) for i in range(3)]
            cwb2 = [men(nc.sbuf_tensor(f"cwb{i}", [128, TH], BF16)) for i in range(2)]
            sgb = men(nc.sbuf_tensor("sgb", [128, TH], BF16))
            t1b = men(nc.sbuf_tensor("t1b", [128, TH], BF16))
            lgs = men(nc.sbuf_tensor("lgs", [16, 512], F32))
            cwl = men(nc.sbuf_tensor("cwl", [128, 4 * E], F32))
            prb = men(nc.sbuf_tensor("prb", [128, E], F32))
            m8b = men(nc.sbuf_tensor("m8b", [128, 8], F32))
            kpb = men(nc.sbuf_tensor("kpb", [128, E], F32))
            Dsc8 = men(nc.sbuf_tensor("Dsc8", [128, NTB * 128], BF16))
            ytv = [men(nc.sbuf_tensor(f"ytv{i}", [128, H], F32)) for i in range(2)]
            ytv_chain = [None, None]
            osc = [men(nc.sbuf_tensor(f"osc{i}", [128, H], F32)) for i in range(2)]
            assert TH <= 1024 and H <= 1024
            pg = men(nc.psum_tensor("pg", [128, max(TH, H)], F32))
            pu = men(nc.psum_tensor("pu", [128, max(TH, H)], F32))
            py = [men(nc.psum_tensor(f"py{i}", [128, TH], F32)) for i in range(2)]

            xm_fm = men(nc.sbuf_tensor("xm_fm", [128, HC * T], BF16))
            xmt = [men(nc.sbuf_tensor("xmt0", [128, H], BF16))] * 2
            xmt_chain = [None]
            free_ev = {"pg": None, "pu": None, "py0": None, "py1": None}
            pbank = {"pg": pg, "pu": pu, "py0": py[0], "py1": py[1]}

            ev_xm_h = [None, None]
            ev_cw_h = [None, None]
            ev_pu_free0 = None
            xmt_free = [None, None]
            ln3_state = {"v3": None, "a3": [None, None]}

            def emit_ln3_stats(lo, hi, hf):
                for tau in range(lo, hi):
                    ln3_state["v3"] = ln_stats(
                        lambda e, s0, wdt, tau=tau:
                        x2[:, tau * H + s0: tau * H + s0 + wdt],
                        "m3", "v3", tau, [ev_x2[tau]])
                ln3_state["a3"][hf] = rstd_batch(
                    "m3", "v3", "r3", "a3", lo, hi, [ln3_state["v3"]])

            def emit_ln3_tile(tau, hf):
                nonlocal ev_pu_free0
                xi = tau % 2
                evx = P.add("scalar", lambda e, tau=tau, xi=xi: e.activation(
                    xmt[xi][:], x2[:, tau * H:(tau + 1) * H], AF.Identity,
                    bias=stat["a3"][:, tau:tau + 1],
                    scale=stat["r3"][:, tau:tau + 1]),
                    waits=[ln3_state["a3"][hf], xmt_chain[0]] + bar, inc="s")

                def xtr(e, xi=xi):
                    pub = pu[:, 0:512].bitcast(BF16)
                    last = None
                    for c in range(HC):
                        last = e.transpose(
                            pub[:, c * 128:(c + 1) * 128],
                            xmt[xi][:, c * 128:(c + 1) * 128], idb[:])
                    return last
                evt = P.add("tensor", xtr,
                            waits=[evx, ev_idb, ev_pu_free0, free_ev["pu"]],
                            inc="t")
                xm3 = xm_fm.rearrange("p (c t) -> p c t", t=T)
                ev_pu_free0 = P.add("vector", lambda e, tau=tau: e.tensor_copy(
                    xm3[:, :, tau * 128:(tau + 1) * 128],
                    pu[:, 0:512].bitcast(BF16)[:, 0:HC * 128]),
                    waits=[evt] + bar, inc="v")
                free_ev["pu"] = ev_pu_free0
                xmt_chain[0] = ev_pu_free0
                ev_xm_h[hf] = ev_pu_free0

            rt_state = {"lgs_free": None, "cwt": None, "s_smx2": None,
                        "s_cwl": None}

            def emit_router_mm(g):
                def lmm(e, g=g):
                    last = None
                    for k in range(HC):
                        last = e.matmul(
                            pg[0:E, 0:512], wrt[:, k * E:(k + 1) * E],
                            xm_fm[:, k * T + g * 512: k * T + (g + 1) * 512],
                            start=(k == 0), stop=(k == HC - 1))
                    return last
                hf = (g * 512) // TH
                evl = P.add("tensor", lmm,
                            waits=[ev_wrt, ev_xm_h[hf], rt_state["lgs_free"],
                                   free_ev["pg"]], inc="t")
                evlg = P.add("vector", lambda e: e.tensor_copy(
                    lgs[0:E, :], pg[0:E, 0:512]), waits=[evl], inc="v")
                free_ev["pg"] = evlg
                P.drain("vector")

                def rtr(e):
                    last = None
                    for j in range(4):
                        last = e.transpose(pu[:, j * E:(j + 1) * E],
                                           lgs[0:E, j * 128:(j + 1) * 128],
                                           idf[0:E, 0:E])
                    return last
                evt = P.add("tensor", rtr,
                            waits=[evlg, rt_state["cwt"], free_ev["pu"]],
                            inc="t")
                rt_state["lgs_free"] = evt
                evc = P.add("vector", lambda e: e.tensor_copy(
                    cwl[:], pu[:, 0:4 * E]), waits=[evt, rt_state["s_cwl"]],
                    inc="v")
                free_ev["pu"] = evc
                rt_state["cwt"] = evc
                P.drain("vector")

            def emit_router_cw(g, j, hf):
                tau = g * 4 + j
                P.add("vector", lambda e, j=j: e.reduce_max(
                    smx[:, 0:1], cwl[:, j * E:(j + 1) * E], axis=AX),
                    waits=[rt_state["cwt"]])
                P.drain("vector")
                evn = P.add("vector", lambda e: e.tensor_scalar(
                    smx[:, 1:2], smx[:, 0:1], -1.0, None, op0=A.mult),
                    waits=[rt_state["s_smx2"]], inc="v")
                eve = P.add("scalar", lambda e, j=j: e.activation(
                    prb[:], cwl[:, j * E:(j + 1) * E], AF.Exp, bias=smx[:, 1:2],
                    accum_out=smx[:, 2:3]), waits=[evn], inc="s")
                rt_state["s_cwl"] = eve
                rt_state["s_smx2"] = eve
                P.add("vector", lambda e: e.reciprocal(
                    smx[:, 3:4], smx[:, 2:3]), waits=[eve])
                P.drain("vector")
                P.add("vector", lambda e: e.tensor_scalar(
                    prb[:], prb[:], smx[:, 3:4], None, op0=A.mult))
                P.drain("vector")
                P.add("vector", lambda e: e.max(out=m8b[:], in_=prb[:]))
                P.drain("vector")
                P.add("vector", lambda e: e.tensor_scalar(
                    kpb[:], prb[:], m8b[:, 1:2], None, op0=A.is_ge))
                P.drain("vector")
                P.add("vector", lambda e: e.tensor_tensor(
                    kpb[:], kpb[:], prb[:], op=A.mult))
                P.drain("vector")
                P.add("vector", lambda e: e.tensor_tensor(
                    smx[:, 4:5], m8b[:, 0:1], m8b[:, 1:2], op=A.add))
                P.drain("vector")
                P.add("vector", lambda e: e.tensor_scalar(
                    smx[:, 4:5], smx[:, 4:5], 1e-20, None, op0=A.add))
                P.drain("vector")
                P.add("vector", lambda e: e.reciprocal(
                    smx[:, 5:6], smx[:, 4:5]), inc="v")
                P.drain("vector")
                ev = P.add("vector", lambda e, tau=tau: e.tensor_scalar(
                    cw_all[:, tau * E:(tau + 1) * E], kpb[:],
                    smx[:, 5:6], None, op0=A.mult), inc="v")
                P.drain("vector")
                ev_cw_h[hf] = ev

            def ln3_router_thunks(hf):
                lo = hf * NTB
                th_list = []
                th_list.append(lambda: emit_ln3_stats(lo, lo + NTB, hf))
                for tau in range(lo, lo + NTB):
                    th_list.append(lambda tau=tau: emit_ln3_tile(tau, hf))
                for g in range(lo // 4, (lo + NTB) // 4):
                    th_list.append(lambda g=g: emit_router_mm(g))
                    for j in range(4):
                        th_list.append(lambda g=g, j=j: emit_router_cw(g, j, hf))
                return th_list

            NE = E + 1
            wgb_free = [None, None, None]
            wub_free = [None, None, None]
            wdb_free = [None, None, None]
            gname = ["pg", "py0"]
            uname = ["pu", "py1"]
            pyname = ["py0", "py1"]
            ev_cwb_ready = [None, None]
            ev_cwb_cons2 = [None, None]
            ev_sgb_cons = None
            ev_t1b_cons = None
            ev_dsc_cons = None
            ev_out_dma = []
            nload = 0

            def emit_cwb_d(eidx, th):
                nonlocal ev_dsc_cons
                evd = None
                for blk in range(NTB):
                    tau = th * NTB + blk
                    evd = P.add("vector", lambda e, tau=tau, eidx=eidx, blk=blk:
                                e.tensor_scalar(
                        Dsc8[:, blk * 128:(blk + 1) * 128], idb[:],
                        cw_all[:, tau * E + eidx: tau * E + eidx + 1],
                        None, op0=A.mult),
                        waits=[ev_cw_h[th], ev_idb, ev_dsc_cons],
                        inc="v" if blk == NTB - 1 else None)
                return evd

            def emit_cwb_m(eidx, bankname, evd):
                nonlocal ev_dsc_cons
                bank = pbank[bankname]
                ri = eidx % 2

                def cwmm(e, bank=bank):
                    last = None
                    for blk in range(NTB):
                        last = e.matmul(
                            bank[:, blk * 128:(blk + 1) * 128], onesb[:],
                            Dsc8[:, blk * 128:(blk + 1) * 128],
                            start=True, stop=True)
                    return last
                evm = P.add("tensor", cwmm,
                            waits=[evd, ev_onesb, free_ev[bankname]], inc="t")
                ev_dsc_cons = evm
                ev = P.add("vector", lambda e, ri=ri, bank=bank: e.tensor_copy(
                    cwb2[ri][:], bank[:, 0:NTB * 128]),
                    waits=[evm, ev_cwb_cons2[ri]], inc="v")
                free_ev[bankname] = ev
                P.drain("vector")
                ev_cwb_ready[ri] = ev

            def emit_cwb(eidx, th, bankname):
                emit_cwb_m(eidx, bankname, emit_cwb_d(eidx, th))

            for th_fn in ln3_router_thunks(0):
                th_fn()
            moe_pump = iter(ln3_router_thunks(1))

            for th in range(2):
                tof = th * TH
                ev_y = [None] * HC
                if th == 1:
                    for th_fn in moe_pump:
                        th_fn()
                emit_cwb(0, th, "pg")
                for ei in range(NE):
                    routed = ei < E
                    ri = ei % 2

                    ev_h = None
                    evd_next = [None]
                    evld_pre = {}
                    for ic in range(IC):
                        sl = nload % 3
                        gn, un = gname[ic % 2], uname[ic % 2]
                        gb, ub = pbank[gn], pbank[un]
                        if routed:
                            src_g = d_wg[ei, :, ic * 128:(ic + 1) * 128]
                            src_u = d_wu[ei, :, ic * 128:(ic + 1) * 128]
                        else:
                            src_g = d_wgs[:, ic * 128:(ic + 1) * 128]
                            src_u = d_wus[:, ic * 128:(ic + 1) * 128]
                        evlg_ = P.add("gpsimd", lambda e, sl=sl, src_g=src_g: e.dma_start(
                            out=wgb[sl][:],
                            in_=src_g.rearrange("(c p) j -> p c j", p=128)),
                            waits=[wgb_free[sl]] + bar, inc=f"dga{sl}", inc_amt=16)
                        evlu_ = P.add("gpsimd", lambda e, sl=sl, src_u=src_u: e.dma_start(
                            out=wub[sl][:],
                            in_=src_u.rearrange("(c p) j -> p c j", p=128)),
                            waits=[wub_free[sl]] + bar, inc=f"dgb{sl}", inc_amt=16)
                        nload += 1

                        def gmm(e, sl=sl, tof=tof, gb=gb):
                            last = None
                            for c in range(HC):
                                for gg in range(NTG):
                                    last = e.matmul(
                                        gb[:, gg * 512:(gg + 1) * 512],
                                        wgb[sl][:, c * 128:(c + 1) * 128],
                                        xm_fm[:, c * T + tof + gg * 512:
                                              c * T + tof + (gg + 1) * 512],
                                        start=(c == 0), stop=(c == HC - 1))
                            return last
                        evg = P.add("tensor", gmm,
                                    waits=[evlg_, ev_xm_h[th], free_ev[gn]],
                                    inc="t")
                        wgb_free[sl] = evg

                        def umm(e, sl=sl, tof=tof, ub=ub):
                            last = None
                            for c in range(HC):
                                for gg in range(NTG):
                                    last = e.matmul(
                                        ub[:, gg * 512:(gg + 1) * 512],
                                        wub[sl][:, c * 128:(c + 1) * 128],
                                        xm_fm[:, c * T + tof + gg * 512:
                                              c * T + tof + (gg + 1) * 512],
                                        start=(c == 0), stop=(c == HC - 1))
                            return last
                        evu = P.add("tensor", umm,
                                    waits=[evlu_, ev_xm_h[th], free_ev[un]],
                                    inc="t")
                        wub_free[sl] = evu
                        evsg = P.add("scalar", lambda e, gb=gb: e.activation(
                            sgb[:], gb[:, 0:TH], AF.Silu),
                            waits=[evg, ev_sgb_cons], inc="s")
                        free_ev[gn] = evsg
                        evt1 = P.add("vector", lambda e, ub=ub: e.tensor_tensor(
                            t1b[:], sgb[:], ub[:, 0:TH], op=A.mult),
                            waits=[evsg, evu, ev_t1b_cons], inc="v")
                        ev_sgb_cons = evt1
                        free_ev[un] = evt1
                        P.drain("vector")
                        if routed:
                            ev_h = P.add("vector", lambda e, ic=ic, ri=ri:
                                         e.tensor_tensor(
                                hpr[:, ic * TH:(ic + 1) * TH], t1b[:], cwb2[ri][:],
                                op=A.mult),
                                waits=[evt1, ev_cwb_ready[ri]], inc="v")
                            ev_cwb_cons2[ri] = ev_h
                        else:
                            ev_h = P.add("vector", lambda e, ic=ic: e.tensor_copy(
                                hpr[:, ic * TH:(ic + 1) * TH], t1b[:]),
                                waits=[evt1], inc="v")
                        ev_t1b_cons = ev_h
                        P.drain("vector")
                        if ic == IC - 2 and routed and ei + 1 < E:
                            evd_next[0] = emit_cwb_d(ei + 1, th)
                        if IC >= 4 and ic >= IC - 3:
                            cpre = ic - (IC - 3)
                            if cpre < HC and cpre not in evld_pre:
                                sl2 = cpre % 3
                                if routed:
                                    src_d2 = d_wd[ei, :, cpre * 128:(cpre + 1) * 128]
                                else:
                                    src_d2 = d_wds[:, cpre * 128:(cpre + 1) * 128]
                                evld_pre[cpre] = P.add(
                                    "gpsimd", lambda e, sl2=sl2, src_d2=src_d2:
                                    e.dma_start(
                                        out=wdb[sl2][:],
                                        in_=src_d2.rearrange(
                                            "(ic p) j -> p ic j", p=128)),
                                    waits=[wdb_free[sl2]] + bar,
                                    inc=f"dgd{sl2}", inc_amt=16)

                    for c in range(HC):
                        sl = c % 3
                        pyn = pyname[c % 2]
                        if routed:
                            src_d = d_wd[ei, :, c * 128:(c + 1) * 128]
                        else:
                            src_d = d_wds[:, c * 128:(c + 1) * 128]
                        if c in evld_pre:
                            evld = evld_pre[c]
                        else:
                            evld = P.add("gpsimd", lambda e, sl=sl, src_d=src_d:
                                         e.dma_start(
                                out=wdb[sl][:],
                                in_=src_d.rearrange("(ic p) j -> p ic j", p=128)),
                                waits=[wdb_free[sl]] + bar, inc=f"dgd{sl}",
                                inc_amt=16)

                        def dmm(e, sl=sl, c=c, pyn=pyn):
                            last = None
                            for ic in range(IC):
                                for gg in range(NTG):
                                    last = e.matmul(
                                        pbank[pyn][:, gg * 512:(gg + 1) * 512],
                                        wdb[sl][:, ic * 128:(ic + 1) * 128],
                                        hpr[:, ic * TH + gg * 512:
                                            ic * TH + (gg + 1) * 512],
                                        start=(ic == 0), stop=(ic == IC - 1))
                            return last
                        evdm = P.add("tensor", dmm,
                                     waits=[evld, ev_h, free_ev[pyn]], inc="t")
                        wdb_free[sl] = evdm
                        if ei == 0:
                            ev_y[c] = P.add("vector", lambda e, c=c, pyn=pyn:
                                            e.tensor_copy(
                                yfm[:, c * TH:(c + 1) * TH],
                                pbank[pyn][:, 0:TH]),
                                waits=[evdm, ev_y[c]], inc="v")
                        else:
                            ev_y[c] = P.add("vector", lambda e, c=c, pyn=pyn:
                                            e.scalar_tensor_tensor(
                                yfm[:, c * TH:(c + 1) * TH], pbank[pyn][:, 0:TH],
                                1.0, yfm[:, c * TH:(c + 1) * TH],
                                op0=A.mult, op1=A.add),
                                waits=[evdm, ev_y[c]], inc="v")
                        free_ev[pyn] = ev_y[c]
                        P.drain("vector")
                        if c == 0 and routed and ei + 1 < E:
                            if evd_next[0] is None:
                                evd_next[0] = emit_cwb_d(ei + 1, th)
                            emit_cwb_m(ei + 1, "pg", evd_next[0])
                        if th == 0 and c in (2, 4, 6):
                            th_fn = next(moe_pump, None)
                            if th_fn is not None:
                                th_fn()

                for blk in range(NTB):
                    tau = th * NTB + blk
                    obk = "pg" if blk % 2 == 0 else "pu"
                    ob = pbank[obk]
                    ev_yt = None
                    for c in range(HC):
                        ev_yt = P.add("tensor", lambda e, c=c, blk=blk, ob=ob:
                                      e.transpose(
                            ob[:, c * 128:(c + 1) * 128],
                            yfm[:, c * TH + blk * 128: c * TH + (blk + 1) * 128],
                            idf[:]),
                            waits=[ev_y[c], ev_id, free_ev[obk]], inc="t")
                    free_ev[obk] = ev_yt
                    po = 4 * (blk % 2)
                    for gi in range((H + 511) // 512):
                        s0_ = gi * 512
                        wdt = min(512, H - s0_)
                        P.add("vector", lambda e, gi=gi, s0_=s0_, wdt=wdt, ob=ob,
                              blk=blk: e.bn_stats(
                            bnb[:, (blk % 2) * 12 + gi * 6:
                                (blk % 2) * 12 + (gi + 1) * 6],
                            ob[:, s0_: s0_ + wdt]),
                            waits=[ev_yt])
                        P.drain("vector")
                    P.add("vector", lambda e, po=po, blk=blk: e.bn_aggr(
                        agg2[:, po: po + 2],
                        bnb[:, (blk % 2) * 12: (blk % 2) * 12 + 6 * nst]))
                    P.drain("vector")
                    evv = P.add("vector", lambda e, po=po: e.tensor_scalar(
                        agg2[:, po + 2: po + 3], agg2[:, po + 1: po + 2], EPS,
                        None, op0=A.add), inc="v")
                    P.drain("vector")
                    evq = P.add("scalar", lambda e, po=po: e.activation(
                        agg2[:, po + 2: po + 3], agg2[:, po + 2: po + 3],
                        AF.Sqrt), waits=[evv], inc="s")
                    ev_a4 = P.add("vector", lambda e, po=po: e.reciprocal(
                        agg2[:, po + 3: po + 4], agg2[:, po + 2: po + 3]),
                        waits=[evq], inc="v")
                    P.drain("vector")
                    zi = blk % 2
                    evz = P.add("vector", lambda e, zi=zi, po=po, ob=ob:
                                e.tensor_scalar(
                        ytv[zi][:], ob[:, 0:H], agg2[:, po: po + 1],
                        agg2[:, po + 3: po + 4], op0=A.subtract, op1=A.mult),
                        waits=[ev_a4, ytv_chain[zi]], inc="v")
                    free_ev[obk] = evz
                    P.drain("vector")
                    evo = P.add("vector", lambda e, zi=zi, tau=tau: e.tensor_tensor(
                        osc[zi][:], ytv[zi][:], x2[:, tau * H:(tau + 1) * H],
                        op=A.add), waits=[evz] + ev_out_dma[-2:-1], inc="v")
                    ytv_chain[zi] = evo
                    P.drain("vector")
                    ev_out_dma.append(P.add("sync", lambda e, zi=zi, tau=tau: e.dma_start(
                        out=d_out.rearrange("(tt p) h -> p tt h", p=128)[
                            :, tau, :],
                        in_=osc[zi][:]), waits=[evo], inc=f"do{zi}", inc_amt=16))

            P.add("sync", None, waits=[("do0", P.count["do0"]),
                                       ("do1", P.count["do1"])])

        P.replay(en(nc.Block()))
    return nc


_NC_CACHE = {}


def _get_nc(key, cfg):
    if key not in _NC_CACHE:
        _NC_CACHE[key] = build_program(cfg)
    return _NC_CACHE[key]


def make_inputs(cfg, x_b, w1, w2, v, w_router, wg, wu, wd, wgs, wus, wds):
    H, E = cfg.H, cfg.E
    vsp = np.zeros((H, KW * KW), dtype=np.float32)
    for k in range(KW):
        vsp[:, k * KW + k] = v[:, 0]
    iw = np.zeros((128, 448), dtype=np.float32)
    for p in range(128):
        iw[p, p + 160] = 1.0
    ident = np.eye(128, dtype=np.float32)
    ones = np.ones((128, 128), dtype=np.float32)
    return {
        "x": np.ascontiguousarray(x_b),
        "w1": np.ascontiguousarray(w1),
        "w2": np.ascontiguousarray(w2),
        "vsp": vsp,
        "wrt": np.ascontiguousarray(w_router.T),
        "wg": np.ascontiguousarray(wg),
        "wu": np.ascontiguousarray(wu),
        "wd": np.ascontiguousarray(wd),
        "wgs": np.ascontiguousarray(wgs),
        "wus": np.ascontiguousarray(wus),
        "wds": np.ascontiguousarray(wds),
        "iw": iw,
        "ident": ident,
        "ones": ones,
    }


def kernel(L0_attn_inputs, L0_attn_w1, L0_attn_w2, L0_attn_v,
           L0_ffn_w_router, L0_ffn_wg, L0_ffn_wu, L0_ffn_wd,
           L0_ffn_wg_s, L0_ffn_wu_s, L0_ffn_wd_s):
    x = np.asarray(L0_attn_inputs, dtype=np.float32)
    B, T, H = x.shape
    cfg = Cfg(T=T, H=H, I=np.asarray(L0_ffn_wg).shape[2],
              E=np.asarray(L0_ffn_w_router).shape[0])
    nc = _get_nc((T, H, cfg.I, cfg.E), cfg)
    common = dict(
        w1=np.asarray(L0_attn_w1, np.float32), w2=np.asarray(L0_attn_w2, np.float32),
        v=np.asarray(L0_attn_v, np.float32),
        w_router=np.asarray(L0_ffn_w_router, np.float32),
        wg=np.asarray(L0_ffn_wg, np.float32), wu=np.asarray(L0_ffn_wu, np.float32),
        wd=np.asarray(L0_ffn_wd, np.float32),
        wgs=np.asarray(L0_ffn_wg_s, np.float32),
        wus=np.asarray(L0_ffn_wu_s, np.float32),
        wds=np.asarray(L0_ffn_wd_s, np.float32))
    in_maps = [make_inputs(cfg, x[b], **common) for b in range(B)]
    res = run_bass_kernel_spmd(nc, in_maps, core_ids=list(range(B)))
    out = np.stack([np.asarray(res.results[b]["out"]) for b in range(B)], axis=0)
    return out.astype(np.float32)
